# version 08
# baseline (speedup 1.0000x reference)
import numpy as np
import contextlib
import concourse.bass as bass
import concourse.mybir as mybir
from concourse.bass_utils import run_bass_kernel_spmd

F32 = mybir.dt.float32
BF16 = mybir.dt.bfloat16
AF = mybir.ActivationFunctionType
ALU = mybir.AluOpType
AX = mybir.AxisListType.X

T = 2048
D = 1024
DI = 2048
NS = 16
TC = 1024
NCH = T // TC
NDT = DI // 128
DEBUG = False
TRUNC = None


class Chan:

    def __init__(self, nc, ctx, name, step, engine_name):
        self.sem = ctx.enter_context(nc.semaphore(name))
        self.step = step
        self.engine_name = engine_name
        self.n = 0

    def tok(self):
        return (self, self.n)


class SubSem:
    def __init__(self, sem, engine_name):
        self.sem = sem
        self.step = 16
        self.engine_name = engine_name
        self.n = 0

    def tok(self):
        return (self, self.n)


class QPool:

    K = 6

    def __init__(self, nc, ctx, name, engine_name):
        self.engine_name = engine_name
        self.subs = [SubSem(ctx.enter_context(nc.semaphore(f"{name}{i}")),
                            "dma-completion") for i in range(self.K)]
        self.i = 0

    def next(self):
        sub = self.subs[self.i % self.K]
        self.i += 1
        throttle = sub.tok() if sub.n > 0 else None
        sub.n += 1
        return sub, throttle

    def all_toks(self):
        return [s.tok() for s in self.subs if s.n > 0]


class Prog:
    def __init__(self):
        self.ops = []

    def op(self, engine, emit, waits=(), inc=None, inc_n=1):
        ws = tuple((c, v) for (c, v) in waits if v > 0)
        incs = ()
        if inc is not None:
            inc.n += inc_n
            incs = ((inc, inc_n),)
        self.ops.append((engine, ws, emit, incs))
        return inc.tok() if inc else None

    def dma(self, q, emit, waits=()):
        sub, throttle = q.next()
        ws = list(waits)
        if throttle is not None:
            ws.append(throttle)
        ws = tuple((c, v) for (c, v) in ws if v > 0)
        self.ops.append((q.engine_name, ws, emit, ((sub, 1),)))
        return sub.tok()

    def section(self, eng_name):
        own = getattr(self, "own", {}).get(eng_name)

        def _(eng):
            waited = {}
            seen_own = 0
            for engine, ws, emit, incs in self.ops:
                if engine != eng_name:
                    continue
                if own is not None and seen_own > waited.get(id(own), -1):
                    eng.wait_ge(own.sem, own.step * seen_own)
                    waited[id(own)] = seen_own
                for (chn, val) in ws:
                    if chn.engine_name == eng_name:
                        continue
                    if waited.get(id(chn), -1) >= val:
                        continue
                    eng.wait_ge(chn.sem, chn.step * val)
                    waited[id(chn)] = val
                instr = emit(eng)
                for (chn, k) in incs:
                    instr.then_inc(chn.sem, chn.step * k)
                    if chn is own:
                        seen_own += k
        return _

    def run(self, block):
        for name in ("sync", "gpsimd", "vector", "scalar", "tensor"):
            getattr(block, name)(self.section(name))


def build_program(dbg=False, parks_external=False):
    nc = bass.Bass("TRN2", target_bir_lowering=False)

    def din(name, shape, dt=F32):
        return nc.dram_tensor(name, list(shape), dt, kind="ExternalInput").ap()

    x0_h = din("x0", [T, D])
    winT_h = din("winT", [D, 2 * DI])
    convdiag_h = din("convdiag", [4, NDT, 128, 128])
    ddiag_h = din("ddiag", [NDT, 128, 128])
    ident_h = din("ident", [128, 128])
    sel16_h = din("sel16", [NS, NS, 128])
    xpjT_h = din("xpjT", [DI, 128])
    dtT_h = din("dtT", [64, DI])
    woutT_h = din("woutT", [DI, D])
    alog_h = din("alog", [DI, NS])
    convb_h = din("convb", [DI])
    dtb_h = din("dtb", [DI])
    wbase_h = din("wbase", [D, D])
    wd1_h = din("wd1", [8, D, 256])
    wd2T_h = din("wd2T", [8, D, 256])
    wad_h = din("wad", [8, D, 64])
    wau_h = din("wau", [8, 64, D])
    bbase_h = din("bbase", [D])
    bd1_h = din("bd1", [8, 256])
    bd2_h = din("bd2", [8, D])
    bad_h = din("bad", [8, 64])
    bau_h = din("bau", [8, D])

    out_h = nc.dram_tensor("out", [T, D], F32, kind="ExternalOutput").ap()
    pk = "ExternalOutput" if parks_external else "Internal"
    u_park = nc.dram_tensor("u_park", [DI, T], BF16, kind=pk).ap()
    g_park = nc.dram_tensor("g_park", [DI, T], BF16, kind=pk).ap()
    y2_park = nc.dram_tensor("y2_park", [DI, T], BF16, kind=pk).ap()
    x1_park = nc.dram_tensor("x1_park", [T, D], BF16, kind=pk).ap()
    xln_park = nc.dram_tensor("xln_park", [D, T], BF16, kind=pk).ap()

    dbg_list = []

    with contextlib.ExitStack() as ctx:
        def sb(name, shape, dt):
            return ctx.enter_context(nc.sbuf_tensor(name, list(shape), dt))

        def psum(name, shape, dt=F32):
            return ctx.enter_context(nc.psum_tensor(name, list(shape), dt))

        p = Prog()
        DMA = QPool(nc, ctx, "c_dma", "sync")
        DMG = QPool(nc, ctx, "c_dmg", "gpsimd")
        DVE = Chan(nc, ctx, "c_dve", 1, "vector")
        ACT = Chan(nc, ctx, "c_act", 1, "scalar")
        PE = Chan(nc, ctx, "c_pe", 1, "tensor")
        ST = QPool(nc, ctx, "c_st", "sync")
        GPS = Chan(nc, ctx, "c_gps", 1, "gpsimd")
        p.own = {"vector": DVE, "scalar": ACT, "gpsimd": GPS}

        def dve(emit, waits=(), inc=DVE):
            return p.op("vector", emit, waits=waits, inc=inc)

        def act(emit, waits=(), inc=ACT):
            return p.op("scalar", emit, waits=waits, inc=inc)

        def gps(emit, waits=(), inc=GPS):
            return p.op("gpsimd", emit, waits=waits, inc=inc)

        def pe(emit, waits=(), inc=None):
            return p.op("tensor", emit, waits=waits, inc=inc)

        def load(dst, src, q=DMA, waits=()):
            return p.dma(q, lambda e, d=dst, s=src:
                         e.dma_start(out=d, in_=s), waits=waits)

        def store(dst, src, waits=(), q_eng=None, ch=ST):
            return p.dma(ch, lambda e, d=dst, s=src:
                         e.dma_start(out=d, in_=s), waits=waits)

        def mm(out, lhsT, rhs, start, stop, waits=(), inc=None):
            return p.op("tensor", lambda e, o=out, l=lhsT, r=rhs, a=start, b=stop:
                        e.matmul(out=o, lhsT=l, rhs=r, start=a, stop=b),
                        waits=waits, inc=inc)

        def tr(out, in_, identity, waits=(), inc=None):
            return p.op("tensor", lambda e, o=out, i=in_, idn=identity:
                        e.transpose(out=o, in_=i, identity=idn),
                        waits=waits, inc=inc)

        def dbg_store(name, src, shape, dt=F32):
            if not dbg:
                return
            h = nc.dram_tensor("dbg_" + name, list(shape), dt,
                               kind="ExternalOutput").ap()
            dbg_list.append("dbg_" + name)
            store(h, src, waits=[DVE.tok(), ACT.tok(), PE.tok()])

        ident_b = sb("ident_b", [128, 128], BF16)
        ones_b = sb("ones_b", [128, 128], BF16)
        ones_f = sb("ones_f", [128, 128], F32)
        sel_b = sb("sel_b", [NS, NS * 128], BF16)
        xpj_b = sb("xpj_b", [128, NDT * 128], BF16)
        dtw_f = sb("dtw_f", [64, DI], F32)
        acol = sb("acol", [128, NDT * NS], F32)
        cb_col = sb("cb_col", [128, NDT], F32)
        dtb_col = sb("dtb_col", [128, NDT], F32)
        ddiag_b = sb("ddiag_b", [128, NDT * 128], BF16)
        stage_f = sb("stage_f", [128, 2 * 512], F32)
        stage2_f = sb("stage2_f", [128, 2 * 512], F32)

        ps_a = psum("ps_a", [128, 512], F32)
        ps_conv = psum("ps_conv", [128, 512], F32)
        ps_b = psum("ps_b", [128, 1024], F32)
        ps_y = psum("ps_y", [128, 1024], F32)
        ps_tm = psum("ps_tm", [128, 1024], BF16)

        blk = ctx.enter_context(nc.Block())

        def load_cast(dst, src, stg, cols, pdim=128):
            t0 = load(stg[0:pdim, 0:cols], src, waits=[DVE.tok()])
            return dve(lambda e, d=dst, s=stg[0:pdim, 0:cols]:
                       e.tensor_copy(out=d, in_=s), waits=[t0])

        const_f = sb("const_f", [128, 4], F32)
        dve(lambda e: e.memset(const_f[:, 0:1], 1.0))
        dve(lambda e: e.memset(const_f[:, 1:2], -1.0))
        dve(lambda e: e.memset(const_f[:, 2:3], -0.5))
        dve(lambda e: e.memset(const_f[:, 3:4], 1e-5))
        nc.const_aps.register(F32, 1.0, const_f[:, 0:1])
        nc.const_aps.register(F32, -1.0, const_f[:, 1:2])
        nc.const_aps.register(F32, -0.5, const_f[:, 2:3])
        nc.const_aps.register(F32, 1e-5, const_f[:, 3:4])
        load_cast(ident_b[:, :], ident_h, stage_f, 128)
        dve(lambda e: e.memset(ones_b[:, :], 1.0))
        dve(lambda e: e.memset(ones_f[:, :], 1.0))
        for n in range(NS):
            load_cast(sel_b[0:NS, n * 128:(n + 1) * 128], sel16_h[n], stage2_f,
                      128, pdim=NS)
        for k in range(NDT):
            load_cast(xpj_b[:, k * 128:(k + 1) * 128],
                      xpjT_h[k * 128:(k + 1) * 128, :], stage_f, 128)
        load(dtw_f[:, :], dtT_h)
        for j in range(NDT):
            t0 = load(stage_f[:, 0:NS], alog_h[j * 128:(j + 1) * 128, :],
                      waits=[ACT.tok(), DVE.tok()])
            t1 = act(lambda e, j=j: e.activation(out=stage2_f[:, 0:NS],
                     in_=stage_f[:, 0:NS], func=AF.Exp), waits=[t0])
            dve(lambda e, j=j: e.tensor_scalar_mul(
                out=acol[:, j * NS:(j + 1) * NS], in0=stage2_f[:, 0:NS],
                scalar1=-1.0), waits=[t1])
        for j in range(NDT):
            load(cb_col[:, j:j + 1],
                 convb_h[j * 128:(j + 1) * 128].rearrange("(a b) -> a b", b=1))
            load(dtb_col[:, j:j + 1],
                 dtb_h[j * 128:(j + 1) * 128].rearrange("(a b) -> a b", b=1))
        for j in range(NDT):
            load_cast(ddiag_b[:, j * 128:(j + 1) * 128], ddiag_h[j], stage_f, 128)

        with contextlib.ExitStack() as mctx:
            def msb(name, shape, dt):
                return mctx.enter_context(nc.sbuf_tensor(name, list(shape), dt))

            x0_tt = msb("x0_tt", [128, 2 * D], F32)
            st_c = msb("st_c", [128, 16], F32)
            xlnT = msb("xlnT", [128, 8 * TC], BF16)
            win_bb = msb("win_bb", [128, 8 * 512], BF16)
            convd_t = msb("convd_t", [128, 2 * 4 * 128], BF16)
            xpad = msb("xpad", [128, 2 * (TC + 3)], BF16)
            tails = msb("tails", [128, NDT * 3], BF16)
            u_t = msb("u_t", [128, 2 * TC], BF16)
            g_t = msb("g_t", [128, 2 * TC], BF16)
            dtlo = msb("dtlo", [64, TC], F32)
            b_rawb = msb("b_rawb", [16, TC], BF16)
            c_rawb = msb("c_rawb", [16, TC], BF16)
            brep = msb("brep", [128, NS * TC], BF16)
            crep = msb("crep", [128, NS * TC], BF16)
            delta = msb("delta", [128, TC], F32)
            sp_t1 = msb("sp_t1", [128, TC], F32)
            sp_t2 = msb("sp_t2", [128, TC], F32)
            u_rl = msb("u_rl", [128, 2 * TC], BF16)
            g_rl = msb("g_rl", [128, 2 * TC], BF16)
            v_t = msb("v_t", [128, TC], BF16)
            da_t = msb("da_t", [128, 2 * TC], F32)
            w_t = msb("w_t", [128, 2 * TC], BF16)
            h_t = msb("h_t", [128, 2 * TC], BF16)
            hc_t = msb("hc_t", [128, 2 * TC], BF16)
            hstate = msb("hstate", [128, NDT * NS], F32)
            y2_t = msb("y2_t", [128, 2 * TC], BF16)
            y2tl = msb("y2tl", [128, 2 * 128], BF16)
            xl_fm = msb("xl_fm", [128, D], BF16)
            o_tm = msb("o_tm", [128, D], BF16)
            x1_tt = msb("x1_tt", [128, D], BF16)
            xl_tt = msb("xl_tt", [128, D], BF16)

            dve(lambda e: e.memset(hstate[:, :], 0.0))
            dve(lambda e: e.memset(tails[:, :], 0.0))

            xcons = [None, None]
            u_slot_prev = [None, None]
            g_slot_prev = [None, None]
            y2_slot_prev = [None, None]
            x1st_prev = None
            prev_sc = None
            prev_thc = None
            trc_hist = [None, None]
            for c in range(NCH):
                t0c = c * TC
                ntt = TC // 128

                xln_done = []
                for tt in range(ntt):
                    slot = tt % 2
                    w0 = [xcons[slot]] if xcons[slot] else []
                    tok = load(x0_tt[:, slot * D:(slot + 1) * D],
                               x0_h[t0c + tt * 128: t0c + (tt + 1) * 128, :],
                               waits=w0)
                    src = x0_tt[:, slot * D:(slot + 1) * D]
                    dve(lambda e, s=src: e.reduce_sum(out=st_c[:, 0:1], in_=s,
                        axis=AX), waits=[tok])
                    act(lambda e, s=src: e.activation(out=stage2_f[:, 0:1024], in_=s,
                        func=AF.Square, accum_out=st_c[:, 1:2]), waits=[tok])
                    dve(lambda e: e.tensor_scalar_mul(out=st_c[:, 2:3],
                        in0=st_c[:, 0:1], scalar1=1.0 / D), waits=[ACT.tok()])
                    dve(lambda e: e.tensor_scalar_mul(out=st_c[:, 3:4],
                        in0=st_c[:, 1:2], scalar1=1.0 / D))
                    dve(lambda e: e.tensor_mul(out=st_c[:, 4:5], in0=st_c[:, 2:3],
                        in1=st_c[:, 2:3]))
                    tv = dve(lambda e: e.tensor_sub(out=st_c[:, 3:4],
                             in0=st_c[:, 3:4], in1=st_c[:, 4:5]))
                    ta = act(lambda e: e.activation(out=st_c[:, 3:4],
                             in_=st_c[:, 3:4], func=AF.Ln, bias=1e-5), waits=[tv])
                    ta = act(lambda e: e.activation(out=st_c[:, 3:4],
                             in_=st_c[:, 3:4], func=AF.Exp, scale=-0.5))
                    tap = dve(lambda e, s=src: e.tensor_scalar(out=o_tm[:, :],
                              in0=s, scalar1=st_c[:, 2:3], scalar2=st_c[:, 3:4],
                              op0=ALU.subtract, op1=ALU.mult), waits=[ta])
                    xcons[slot] = tap
                    for j in range(8):
                        tt_tok = tr(ps_tm[:, j * 128:(j + 1) * 128],
                                    o_tm[0:128, j * 128:(j + 1) * 128],
                                    ident_b[:, :],
                                    waits=([tap, DVE.tok()] + ST.all_toks()) if j == 0 else [],
                                    inc=PE if j == 7 else None)
                    for j in range(8):
                        xd = dve(lambda e, j=j, tt=tt: e.tensor_copy(
                            out=xlnT[:, j * TC + tt * 128: j * TC + (tt + 1) * 128],
                            in_=ps_tm[:, j * 128:(j + 1) * 128]),
                            waits=[tt_tok] if j == 0 else [])
                    xln_done.append(xd)
                txln = DVE.tok()
                if c == 0:
                    dbg_store("xlnT_c0", xlnT[:, :], [128, 8 * TC], BF16)
                if c == 1:
                    dbg_store("xlnT_c1", xlnT[:, :], [128, 8 * TC], BF16)

                uc = [None] * NDT
                xp_done = None
                conv_loaded = [None, None]
                for m in range(NDT):
                    mg, mi = divmod(m, 4)
                    mslot = m % 2
                    if mi == 0:
                        wb_toks = []
                        for k in range(8):
                            slot = k % 2
                            t0 = load(stage_f[:, slot * 512:(slot + 1) * 512],
                                      winT_h[k * 128:(k + 1) * 128,
                                             mg * 512:(mg + 1) * 512],
                                      q=DMG, waits=[DVE.tok()])
                            wb_toks.append(dve(lambda e, k=k, slot=slot:
                                e.tensor_copy(out=win_bb[:, k * 512:(k + 1) * 512],
                                in_=stage_f[:, slot * 512:(slot + 1) * 512]),
                                waits=[t0, PE.tok()]))
                    ct = []
                    for jj in range(4):
                        t0 = load(stage2_f[:, (jj % 2) * 128:(jj % 2) * 128 + 128],
                                  convdiag_h[jj, m], q=DMG, waits=[DVE.tok()])
                        ct.append(dve(lambda e, jj=jj, mslot=mslot:
                            e.tensor_copy(out=convd_t[:, (mslot * 4 + jj) * 128:
                                          (mslot * 4 + jj + 1) * 128],
                            in_=stage2_f[:, (jj % 2) * 128:(jj % 2) * 128 + 128]),
                            waits=[t0, PE.tok()]))
                    tl0 = dve(lambda e, m=m, mslot=mslot: e.tensor_copy(
                        out=xpad[:, mslot * (TC + 3): mslot * (TC + 3) + 3],
                        in_=tails[:, m * 3:(m + 1) * 3]),
                        waits=[u_slot_prev[mslot]] if u_slot_prev[mslot] else [])
                    for half in range(2):
                        for k in range(8):
                            w0 = ([wb_toks[k], txln, DVE.tok(), ACT.tok()]
                                  if k == 0 else [wb_toks[k]])
                            mm(ps_a[:, :],
                               win_bb[:, k * 512 + mi * 128: k * 512 + (mi + 1) * 128],
                               xlnT[:, k * TC + half * 512: k * TC + half * 512 + 512],
                               k == 0, k == 7, waits=w0,
                               inc=PE if k == 7 else None)
                        dve(lambda e, mslot=mslot, half=half: e.tensor_copy(
                            out=xpad[:, mslot * (TC + 3) + 3 + half * 512:
                                     mslot * (TC + 3) + 3 + half * 512 + 512],
                            in_=ps_a[:, :]), waits=[PE.tok(), tl0])
                    tev = DVE.tok()
                    dve(lambda e, m=m, mslot=mslot: e.tensor_copy(
                        out=tails[:, m * 3:(m + 1) * 3],
                        in_=xpad[:, mslot * (TC + 3) + TC: mslot * (TC + 3) + TC + 3]))
                    for half in range(2):
                        for jj in range(4):
                            mm(ps_conv[:, :],
                               convd_t[:, (mslot * 4 + jj) * 128:
                                       (mslot * 4 + jj + 1) * 128],
                               xpad[:, mslot * (TC + 3) + jj + half * 512:
                                    mslot * (TC + 3) + jj + half * 512 + 512],
                               jj == 0, jj == 3,
                               waits=([tev] + ct if jj == 0 else []) +
                                     ([ACT.tok()] if jj == 0 else []),
                               inc=PE if jj == 3 else None)
                        act(lambda e, mslot=mslot, half=half, m=m:
                            e.activation(out=u_t[:, mslot * TC + half * 512:
                                                 mslot * TC + half * 512 + 512],
                                         in_=ps_conv[:, :], func=AF.Silu,
                                         bias=cb_col[:, m:m + 1]),
                            waits=[PE.tok()])
                    tu = ACT.tok()
                    for half in range(2):
                        xp_done = mm(ps_b[:, half * 512: half * 512 + 512],
                                     xpj_b[:, m * 128:(m + 1) * 128],
                                     u_t[:, mslot * TC + half * 512:
                                         mslot * TC + half * 512 + 512],
                                     m == 0, m == NDT - 1,
                                     waits=[tu, DVE.tok()] if half == 0 else [],
                                     inc=PE if (m == NDT - 1 and half == 1) else None)
                    uc[m] = store(u_park[m * 128:(m + 1) * 128,
                                         t0c:t0c + TC],
                                  u_t[:, mslot * TC:(mslot + 1) * TC],
                                  waits=[tu], q_eng="gpsimd", ch=DMG)
                    u_slot_prev[mslot] = uc[m]

                gc = [None] * NDT
                for m in range(NDT, 2 * NDT):
                    mg, mi = divmod(m, 4)
                    mslot = m % 2
                    if mi == 0:
                        wb_toks = []
                        for k in range(8):
                            slot = k % 2
                            t0 = load(stage_f[:, slot * 512:(slot + 1) * 512],
                                      winT_h[k * 128:(k + 1) * 128,
                                             mg * 512:(mg + 1) * 512],
                                      q=DMG, waits=[DVE.tok()])
                            wb_toks.append(dve(lambda e, k=k, slot=slot:
                                e.tensor_copy(out=win_bb[:, k * 512:(k + 1) * 512],
                                in_=stage_f[:, slot * 512:(slot + 1) * 512]),
                                waits=[t0, PE.tok()]))
                    for half in range(2):
                        for k in range(8):
                            w0 = [wb_toks[k]]
                            if k == 0:
                                w0 += [DVE.tok(), ACT.tok()]
                                if g_slot_prev[mslot]:
                                    w0.append(g_slot_prev[mslot])
                            mm(ps_a[:, :],
                               win_bb[:, k * 512 + mi * 128: k * 512 + (mi + 1) * 128],
                               xlnT[:, k * TC + half * 512: k * TC + half * 512 + 512],
                               k == 0, k == 7, waits=w0,
                               inc=PE if k == 7 else None)
                        act(lambda e, mslot=mslot, half=half:
                            e.activation(out=g_t[:, mslot * TC + half * 512:
                                                 mslot * TC + half * 512 + 512],
                                         in_=ps_a[:, :], func=AF.Silu),
                            waits=[PE.tok()])
                    tg = ACT.tok()
                    gc[m - NDT] = store(g_park[(m - NDT) * 128:(m - NDT + 1) * 128,
                                               t0c:t0c + TC],
                                        g_t[:, mslot * TC:(mslot + 1) * TC],
                                        waits=[tg], q_eng="gpsimd", ch=DMG)
                    g_slot_prev[mslot] = gc[m - NDT]

                tev1 = dve(lambda e: e.tensor_copy(out=dtlo[:, :], in_=ps_b[0:64, :]),
                           waits=[xp_done])
                dve(lambda e: e.tensor_copy(out=b_rawb[:, :], in_=ps_b[64:80, :]))
                tev2 = dve(lambda e: e.tensor_copy(out=c_rawb[:, :], in_=ps_b[96:112, :]))
                if c == 0:
                    dbg_store("dtlo_c0", dtlo[:, :], [64, TC], F32)
                    dbg_store("braw_c0", b_rawb[:, :], [16, TC], BF16)

                for n in range(NS):
                    for half in range(2):
                        t0 = mm(ps_conv[:, :], sel_b[0:NS, n * 128:(n + 1) * 128],
                                b_rawb[:, half * 512: half * 512 + 512], True, True,
                                waits=[tev2, DVE.tok(), ACT.tok()], inc=PE)
                        dve(lambda e, n=n, half=half: e.tensor_copy(
                            out=brep[:, n * TC + half * 512: n * TC + half * 512 + 512],
                            in_=ps_conv[:, :]), waits=[t0])
                        t0 = mm(ps_conv[:, :], sel_b[0:NS, n * 128:(n + 1) * 128],
                                c_rawb[:, half * 512: half * 512 + 512], True, True,
                                waits=[DVE.tok()], inc=PE)
                        dve(lambda e, n=n, half=half: e.tensor_copy(
                            out=crep[:, n * TC + half * 512: n * TC + half * 512 + 512],
                            in_=ps_conv[:, :]), waits=[t0])
                trep = DVE.tok()

                y2c = [None] * NDT
                urc = [None, None]
                grc = [None, None]
                for j in range(NDT):
                    js = j % 2
                    tur = load(u_rl[:, js * TC:(js + 1) * TC],
                               u_park[j * 128:(j + 1) * 128, t0c:t0c + TC],
                               waits=[uc[j], urc[js]] if urc[js] else [uc[j]])
                    tgr = load(g_rl[:, js * TC:(js + 1) * TC],
                               g_park[j * 128:(j + 1) * 128, t0c:t0c + TC],
                               waits=[gc[j], grc[js]] if grc[js] else [gc[j]])
                    for half in range(2):
                        tdp = mm(ps_b[:, half * 512: half * 512 + 512],
                                 dtw_f[:, j * 128:(j + 1) * 128],
                                 dtlo[:, half * 512: half * 512 + 512], True, True,
                                 waits=[tev1, ACT.tok(), DVE.tok()] if half == 0 else [],
                                 inc=PE if half == 1 else None)
                    txb = dve(lambda e, j=j: e.tensor_scalar(
                              out=sp_t2[:, :], in0=ps_b[:, 0:TC],
                              scalar1=dtb_col[:, j:j + 1], scalar2=None,
                              op0=ALU.add), waits=[tdp, ACT.tok()])
                    tsp0 = dve(lambda e: e.tensor_scalar(
                               out=delta[:, 0:TC], in0=sp_t2[:, :],
                               scalar1=0.0, scalar2=None, op0=ALU.max),
                               waits=[])
                    tsp1 = act(lambda e: e.activation(out=sp_t1[:, :],
                               in_=sp_t2[:, :], func=AF.Abs), waits=[txb])
                    tsp1 = act(lambda e: e.activation(out=sp_t2[:, :], in_=sp_t1[:, :],
                               func=AF.Exp, scale=-1.0))
                    tsp1 = act(lambda e: e.activation(out=sp_t1[:, :], in_=sp_t2[:, :],
                               func=AF.Ln, bias=1.0))
                    tdel = dve(lambda e, js=js: e.tensor_add(
                        out=delta[:, 0:TC],
                        in0=delta[:, 0:TC],
                        in1=sp_t1[:, :]), waits=[tsp0, tsp1])
                    if c == 0 and j == 0:
                        dbg_store("delta_c0", delta[:, 0:TC], [128, TC], F32)
                    tv2 = dve(lambda e, js=js: e.tensor_mul(
                        out=v_t[:, :],
                        in0=delta[:, 0:TC],
                        in1=u_rl[:, js * TC:(js + 1) * TC]), waits=[tur])
                    mm(ps_y[:, 0:512], ddiag_b[:, j * 128:(j + 1) * 128],
                       u_rl[:, js * TC: js * TC + 512], True, False,
                       waits=[tur, DVE.tok()])
                    mm(ps_y[:, 512:1024], ddiag_b[:, j * 128:(j + 1) * 128],
                       u_rl[:, js * TC + 512: js * TC + 1024], True, False, inc=PE)
                    urc[js] = PE.tok()
                    cur_sc = [None] * NS
                    cur_thc = [None] * NS
                    for n in range(NS):
                        asl = n % 2
                        war_da = (cur_sc[n - 2] if n >= 2 else
                                  (prev_sc[NS - 2 + asl] if prev_sc else None))
                        war_h = (cur_thc[n - 2] if n >= 2 else
                                 (prev_thc[NS - 2 + asl] if prev_thc else None))
                        tda = act(lambda e, j=j, n=n, asl=asl:
                                  e.activation(out=da_t[:, asl * TC:(asl + 1) * TC],
                                               in_=delta[:, 0:TC],
                                               func=AF.Exp,
                                               scale=acol[:, j * NS + n: j * NS + n + 1]),
                                  waits=[tdel] + ([war_da] if war_da else []))
                        tw = gps(lambda e, n=n, asl=asl: e.tensor_mul(
                            out=w_t[:, asl * TC:(asl + 1) * TC],
                            in0=v_t[:, :],
                            in1=brep[:, n * TC:(n + 1) * TC]),
                            waits=[trep, tv2, tda] + ([war_da] if war_da else []))
                        tsc = dve(lambda e, j=j, n=n, asl=asl: e.tensor_tensor_scan(
                            out=h_t[:, asl * TC:(asl + 1) * TC],
                            data0=da_t[:, asl * TC:(asl + 1) * TC],
                            data1=w_t[:, asl * TC:(asl + 1) * TC],
                            initial=hstate[:, j * NS + n: j * NS + n + 1],
                            op0=ALU.mult, op1=ALU.add),
                            waits=[tw] + ([war_h] if war_h else []))
                        cur_sc[n] = tsc
                        dve(lambda e, j=j, n=n, asl=asl: e.tensor_copy(
                            out=hstate[:, j * NS + n: j * NS + n + 1],
                            in_=h_t[:, asl * TC + TC - 1: asl * TC + TC]))
                        hc_eng = dve if asl == 0 else gps
                        thc = hc_eng(lambda e, n=n, asl=asl: e.tensor_mul(
                            out=hc_t[:, asl * TC:(asl + 1) * TC],
                            in0=h_t[:, asl * TC:(asl + 1) * TC],
                            in1=crep[:, n * TC:(n + 1) * TC]),
                            waits=[tsc] + ([trc_hist[asl]] if trc_hist[asl] else []))
                        cur_thc[n] = thc
                        mm(ps_y[:, 0:512], ident_b[:, :],
                           hc_t[:, asl * TC: asl * TC + 512], False, n == NS - 1,
                           waits=[thc])
                        trc_hist[asl] = mm(ps_y[:, 512:1024], ident_b[:, :],
                           hc_t[:, asl * TC + 512: asl * TC + 1024],
                           False, n == NS - 1, inc=PE)
                    prev_sc, prev_thc = cur_sc, cur_thc
                    ty2 = dve(lambda e, js=js: e.tensor_mul(
                        out=y2_t[:, js * TC:(js + 1) * TC], in0=ps_y[:, 0:TC],
                        in1=g_rl[:, js * TC:(js + 1) * TC]),
                        waits=[PE.tok(), tgr] +
                              ([y2_slot_prev[js]] if y2_slot_prev[js] else []))
                    grc[js] = ty2
                    y2c[j] = store(y2_park[j * 128:(j + 1) * 128, t0c:t0c + TC],
                                   y2_t[:, js * TC:(js + 1) * TC],
                                   waits=[ty2], q_eng="gpsimd", ch=DMG)
                    y2_slot_prev[js] = y2c[j]
                if c == 0:
                    dbg_store("y2_c0", y2_t[:, 0:TC], [128, TC], BF16)

                wo_toks = []
                for k in range(NDT):
                    for half in range(2):
                        slot = (k * 2 + half) % 2
                        t0 = load(stage_f[:, slot * 512:(slot + 1) * 512],
                                  woutT_h[k * 128:(k + 1) * 128,
                                          half * 512: half * 512 + 512],
                                  q=DMG, waits=[DVE.tok(), trep])
                        wo_toks.append(dve(lambda e, k=k, half=half, slot=slot:
                            e.tensor_copy(out=brep[:, k * TC + half * 512:
                                          k * TC + half * 512 + 512],
                            in_=stage_f[:, slot * 512:(slot + 1) * 512]),
                            waits=[t0, PE.tok()]))
                two = DVE.tok()
                y2lc = [None, None]
                for tt in range(ntt):
                    for k in range(NDT):
                        ksl = k % 2
                        w0 = [y2c[k], two]
                        if k == 0:
                            w0 += [DVE.tok(), ACT.tok()]
                        if y2lc[ksl]:
                            w0.append(y2lc[ksl])
                        tyl = load(y2tl[:, ksl * 128:(ksl + 1) * 128],
                                   y2_park[k * 128:(k + 1) * 128,
                                           t0c + tt * 128: t0c + (tt + 1) * 128],
                                   waits=w0)
                        for eb in range(2):
                            tmm = mm(ps_y[:, eb * 512: eb * 512 + 512],
                                     y2tl[:, ksl * 128:(ksl + 1) * 128],
                                     brep[:, k * TC + eb * 512: k * TC + eb * 512 + 512],
                                     k == 0, k == NDT - 1,
                                     waits=[tyl] if eb == 0 else [],
                                     inc=PE if eb == 1 else None)
                        y2lc[ksl] = tmm
                    top = PE.tok()
                    trd = dve(lambda e: e.reduce_sum(out=st_c[:, 0:1], in_=ps_y[:, 0:D],
                        axis=AX), waits=[top])
                    act(lambda e: e.activation(out=stage2_f[:, 0:1024], in_=ps_y[:, 0:D],
                        func=AF.Square, accum_out=st_c[:, 1:2]), waits=[top, trd])
                    dve(lambda e: e.tensor_scalar_mul(out=st_c[:, 2:3],
                        in0=st_c[:, 0:1], scalar1=1.0 / D), waits=[ACT.tok()])
                    dve(lambda e: e.tensor_scalar_mul(out=st_c[:, 3:4],
                        in0=st_c[:, 1:2], scalar1=1.0 / D))
                    dve(lambda e: e.tensor_mul(out=st_c[:, 4:5], in0=st_c[:, 2:3],
                        in1=st_c[:, 2:3]))
                    tv = dve(lambda e: e.tensor_sub(out=st_c[:, 3:4],
                             in0=st_c[:, 3:4], in1=st_c[:, 4:5]))
                    ta = act(lambda e: e.activation(out=st_c[:, 3:4],
                             in_=st_c[:, 3:4], func=AF.Ln, bias=1e-5), waits=[tv])
                    ta = act(lambda e: e.activation(out=st_c[:, 3:4],
                             in_=st_c[:, 3:4], func=AF.Exp, scale=-0.5))
                    tl2 = dve(lambda e: e.tensor_scalar(out=o_tm[:, :],
                              in0=ps_y[:, 0:D], scalar1=st_c[:, 2:3],
                              scalar2=st_c[:, 3:4], op0=ALU.subtract, op1=ALU.mult),
                              waits=[ta])
                    slot = tt % 2
                    tok = load(x0_tt[:, slot * D:(slot + 1) * D],
                               x0_h[t0c + tt * 128: t0c + (tt + 1) * 128, :],
                               waits=[xcons[slot]] if xcons[slot] else [])
                    tx1 = dve(lambda e, slot=slot: e.tensor_add(out=x1_tt[:, :],
                              in0=x0_tt[:, slot * D:(slot + 1) * D], in1=o_tm[:, :]),
                              waits=[tok, tl2] +
                                    ([x1st_prev] if x1st_prev else []))
                    xcons[slot] = tx1
                    x1st_prev = store(x1_park[t0c + tt * 128: t0c + (tt + 1) * 128, :],
                                      x1_tt[:, :], waits=[tx1], q_eng="gpsimd", ch=DMG)
                    dve(lambda e: e.reduce_sum(out=st_c[:, 0:1], in_=x1_tt[:, :],
                        axis=AX), waits=[tx1])
                    act(lambda e: e.activation(out=stage2_f[:, 0:1024], in_=x1_tt[:, :],
                        func=AF.Square, accum_out=st_c[:, 1:2]), waits=[tx1])
                    dve(lambda e: e.tensor_scalar_mul(out=st_c[:, 2:3],
                        in0=st_c[:, 0:1], scalar1=1.0 / D), waits=[ACT.tok()])
                    dve(lambda e: e.tensor_scalar_mul(out=st_c[:, 3:4],
                        in0=st_c[:, 1:2], scalar1=1.0 / D))
                    dve(lambda e: e.tensor_mul(out=st_c[:, 4:5], in0=st_c[:, 2:3],
                        in1=st_c[:, 2:3]))
                    tv = dve(lambda e: e.tensor_sub(out=st_c[:, 3:4],
                             in0=st_c[:, 3:4], in1=st_c[:, 4:5]))
                    ta = act(lambda e: e.activation(out=st_c[:, 3:4],
                             in_=st_c[:, 3:4], func=AF.Ln, bias=1e-5), waits=[tv])
                    ta = act(lambda e: e.activation(out=st_c[:, 3:4],
                             in_=st_c[:, 3:4], func=AF.Exp, scale=-0.5))
                    tl3 = dve(lambda e: e.tensor_scalar(out=xl_tt[:, :],
                              in0=x1_tt[:, :], scalar1=st_c[:, 2:3],
                              scalar2=st_c[:, 3:4], op0=ALU.subtract, op1=ALU.mult),
                              waits=[ta])
                    for j in range(8):
                        ttr = tr(ps_tm[:, j * 128:(j + 1) * 128],
                                 xl_tt[0:128, j * 128:(j + 1) * 128], ident_b[:, :],
                                 waits=[tl3, DVE.tok()] if j == 0 else [],
                                 inc=PE if j == 7 else None)
                    tfm = dve(lambda e: e.tensor_copy(out=xl_fm[:, :],
                              in_=ps_tm[:, 0:D]), waits=[ttr] + ST.all_toks())
                    for j in range(8):
                        store(xln_park[j * 128:(j + 1) * 128,
                                       t0c + tt * 128: t0c + (tt + 1) * 128],
                              xl_fm[:, j * 128:(j + 1) * 128],
                              waits=[tfm] if j == 0 else [])

        with contextlib.ExitStack() as fctx:
            def fsb(name, shape, dt):
                return fctx.enter_context(nc.sbuf_tensor(name, list(shape), dt))

            x1lnT = fsb("x1lnT", [128, 8 * T], BF16)
            base_tm = fsb("base_tm", [128, 16 * D], BF16)
            wbig = fsb("wbig", [128, 8 * 512], BF16)
            wffn_b = fsb("wffn_b", [128, 8 * 256], BF16)
            wffn2_b = fsb("wffn2_b", [128, 8 * 256], BF16)
            h_e = fsb("h_e", [128, 2 * T], BF16)
            g_e = fsb("g_e", [128, 2 * T], BF16)
            m_e = fsb("m_e", [128, 2 * 256], BF16)
            mh_e = fsb("mh_e", [128, 2 * T], BF16)
            dt_scr = fsb("dt_scr", [128, T], BF16)
            acc = fsb("acc", [128, 64], F32)
            lrow = fsb("lrow", [1, 80], F32)
            mask_c = fsb("mask_c", [128, 8], F32)
            wsel_d = fsb("wsel_d", [128, 8 * 64], F32)
            wsel_db = fsb("wsel_db", [128, 8 * 64], BF16)
            wsel_u = fsb("wsel_u", [65, D], F32)
            wsel_ub = fsb("wsel_ub", [65, D], BF16)
            bcol = fsb("bcol", [128, 12], F32)
            bcolb = fsb("bcolb", [128, 8], BF16)
            bau_acc = fsb("bau_acc", [1, D], F32)
            bau_b = fsb("bau_b", [1, D], BF16)
            bsd = fsb("bsd", [64, 1], F32)
            a_sb = fsb("a_sb", [65, T], BF16)
            o2_tt = fsb("o2_tt", [128, D], BF16)
            st_c2 = fsb("st_c2", [128, 8], F32)
            sq2 = fsb("sq2", [128, D], F32)
            ot_f = fsb("ot_f", [128, 2 * D], F32)
            x1_rl = fsb("x1_rl", [128, 2 * D], BF16)
            bbrow = fsb("bbrow", [1, D], F32)

            tx = []
            for j in range(8):
                tx.append(load(x1lnT[:, j * T:(j + 1) * T],
                               xln_park[j * 128:(j + 1) * 128, :],
                               waits=ST.all_toks() + DMG.all_toks()))
            txl = tx[-1]
            load(bbrow[0:1, :], bbase_h.rearrange("(b a) -> b a", b=1))
            tbb_list = DMA.all_toks()
            tbbb = dve(lambda e: e.tensor_copy(out=dt_scr[0:1, 0:D],
                       in_=bbrow[0:1, :]), waits=tbb_list)

            for eb in range(2):
                wb_toks = []
                for k in range(8):
                    slot = k % 2
                    t0 = load(stage_f[:, slot * 512:(slot + 1) * 512],
                              wbase_h[k * 128:(k + 1) * 128,
                                      eb * 512: eb * 512 + 512],
                              q=DMG, waits=[DVE.tok()])
                    wb_toks.append(dve(lambda e, k=k, slot=slot:
                        e.tensor_copy(out=wbig[:, k * 512:(k + 1) * 512],
                        in_=stage_f[:, slot * 512:(slot + 1) * 512]),
                        waits=[t0, PE.tok()]))
                for tt in range(16):
                    for k in range(8):
                        w0 = [wb_toks[k], txl, DVE.tok()] if k == 0 else [wb_toks[k]]
                        mm(ps_y[:, eb * 512: eb * 512 + 512],
                           x1lnT[:, k * T + tt * 128: k * T + (tt + 1) * 128],
                           wbig[:, k * 512:(k + 1) * 512],
                           k == 0, False, waits=w0)
                    mm(ps_y[:, eb * 512: eb * 512 + 512], ones_b[0:1, 0:128],
                       dt_scr[0:1, eb * 512: eb * 512 + 512], False, True,
                       waits=[tbbb], inc=PE)
                    dve(lambda e, tt=tt, eb=eb: e.tensor_copy(
                        out=base_tm[:, tt * D + eb * 512: tt * D + eb * 512 + 512],
                        in_=ps_y[:, eb * 512: eb * 512 + 512]),
                        waits=[PE.tok()])
            tbase = DVE.tok()

            for j in range(8):
                dve(lambda e, j=j: e.tensor_scalar(out=dt_scr[:, 0:T],
                    in0=x1lnT[:, j * T:(j + 1) * T], scalar1=1.0, scalar2=0.0,
                    op0=ALU.mult, op1=ALU.add, accum_out=acc[:, 48 + j:49 + j]),
                    waits=[txl, tbase])
            tcolx = DVE.tok()

            for ei in range(8):
                for j in range(8):
                    load(bcol[:, j:j + 1],
                         bd2_h[ei, j * 128:(j + 1) * 128].rearrange("(a b) -> a b", b=1),
                         waits=[DVE.tok(), PE.tok()])
                load(bcol[:, 8:9], bd1_h[ei, 0:128].rearrange("(a b) -> a b", b=1))
                load(bcol[:, 9:10], bd1_h[ei, 128:256].rearrange("(a b) -> a b", b=1))
                tbd_list = DMA.all_toks()
                tbdb = dve(lambda e: e.tensor_copy(out=bcolb[:, 0:8],
                           in_=bcol[:, 0:8]), waits=tbd_list)
                wt1, wt2 = [], []
                for k in range(8):
                    s0 = stage_f[:, 0:256] if k % 2 == 0 else stage_f[:, 256:512]
                    t0 = load(s0, wd1_h[ei, k * 128:(k + 1) * 128, :], q=DMG,
                              waits=[DVE.tok()])
                    wt1.append(dve(lambda e, k=k, s0=s0:
                        e.tensor_copy(out=wffn_b[:, k * 256:(k + 1) * 256], in_=s0),
                        waits=[t0, PE.tok()]))
                    s1 = stage2_f[:, 0:256] if k % 2 == 0 else stage2_f[:, 256:512]
                    t0 = load(s1, wd2T_h[ei, k * 128:(k + 1) * 128, :], q=DMG,
                              waits=[DVE.tok()])
                    wt2.append(dve(lambda e, k=k, s1=s1:
                        e.tensor_copy(out=wffn2_b[:, k * 256:(k + 1) * 256], in_=s1),
                        waits=[t0, PE.tok()]))
                for mi in range(2):
                    for half in range(4):
                        for k in range(8):
                            w0 = [wt1[k], DVE.tok()] if k == 0 else [wt1[k]]
                            mm(ps_a[:, :],
                               wffn_b[:, k * 256 + mi * 128: k * 256 + (mi + 1) * 128],
                               x1lnT[:, k * T + half * 512: k * T + half * 512 + 512],
                               k == 0, k == 7, waits=w0,
                               inc=PE if k == 7 else None)
                        dve(lambda e, mi=mi, half=half:
                            e.tensor_scalar(out=h_e[:, mi * T + half * 512:
                                                    mi * T + half * 512 + 512],
                                            in0=ps_a[:, :],
                                            scalar1=bcol[:, 8 + mi: 9 + mi],
                                            scalar2=0.0, op0=ALU.add, op1=ALU.max),
                            waits=[PE.tok()] + tbd_list)
                th = DVE.tok()
                for mi in range(2):
                    dve(lambda e, mi=mi: e.tensor_scalar(out=dt_scr[:, 0:T],
                        in0=h_e[:, mi * T:(mi + 1) * T], scalar1=1.0,
                        scalar2=0.0, op0=ALU.mult, op1=ALU.add,
                        accum_out=acc[:, 40 + mi * 4: 41 + mi * 4]))
                    dve(lambda e, mi=mi: e.memset(
                        acc[:, 41 + mi * 4: 44 + mi * 4], 0.0))
                for mi in range(2):
                    for half in range(4):
                        for k in range(8):
                            w0 = [wt2[k], DVE.tok()] if k == 0 else [wt2[k]]
                            mm(ps_conv[:, :],
                               wffn2_b[:, k * 256 + mi * 128: k * 256 + (mi + 1) * 128],
                               x1lnT[:, k * T + half * 512: k * T + half * 512 + 512],
                               k == 0, k == 7, waits=w0,
                               inc=PE if k == 7 else None)
                        dve(lambda e, mi=mi, half=half:
                            e.tensor_copy(out=g_e[:, mi * T + half * 512:
                                                  mi * T + half * 512 + 512],
                                          in_=ps_conv[:, :]), waits=[PE.tok()])
                tg = DVE.tok()
                for mi in range(2):
                    for k in range(8):
                        w0 = [wt2[k], DVE.tok()] if k == 0 else [wt2[k]]
                        mm(ps_b[:, 0:256],
                           wffn2_b[:, k * 256 + mi * 128: k * 256 + (mi + 1) * 128],
                           wffn2_b[:, k * 256:(k + 1) * 256],
                           k == 0, k == 7, waits=w0, inc=PE if k == 7 else None)
                    dve(lambda e, mi=mi: e.tensor_copy(
                        out=m_e[:, mi * 256:(mi + 1) * 256], in_=ps_b[:, 0:256]),
                        waits=[PE.tok()])
                tme = DVE.tok()
                for mi in range(2):
                    for half in range(4):
                        for k in range(2):
                            w0 = [tme, th, DVE.tok()] if k == 0 else []
                            mm(ps_conv[:, :],
                               m_e[:, k * 256 + mi * 128: k * 256 + (mi + 1) * 128],
                               h_e[:, k * T + half * 512: k * T + half * 512 + 512],
                               k == 0, k == 1, waits=w0, inc=PE if k == 1 else None)
                        dve(lambda e, mi=mi, half=half:
                            e.tensor_copy(out=mh_e[:, mi * T + half * 512:
                                                   mi * T + half * 512 + 512],
                                          in_=ps_conv[:, :]), waits=[PE.tok()])
                tmh = DVE.tok()
                for mi in range(2):
                    dve(lambda e, mi=mi: e.tensor_mul(out=dt_scr[:, 0:T],
                        in0=h_e[:, mi * T:(mi + 1) * T],
                        in1=mh_e[:, mi * T:(mi + 1) * T]), waits=[tmh])
                    dve(lambda e, mi=mi: e.tensor_scalar(out=dt_scr[:, 0:T],
                        in0=dt_scr[:, 0:T], scalar1=1.0, scalar2=0.0,
                        op0=ALU.mult, op1=ALU.add, accum_out=acc[:, 0 + mi:1 + mi]))
                    dve(lambda e, mi=mi: e.tensor_mul(out=dt_scr[:, 0:T],
                        in0=h_e[:, mi * T:(mi + 1) * T],
                        in1=g_e[:, mi * T:(mi + 1) * T]), waits=[tg])
                    dve(lambda e, mi=mi: e.tensor_scalar(out=dt_scr[:, 0:T],
                        in0=dt_scr[:, 0:T], scalar1=1.0, scalar2=0.0,
                        op0=ALU.mult, op1=ALU.add, accum_out=acc[:, 2 + mi:3 + mi]))
                for mi in range(2):
                    for k in range(8):
                        w0 = [tbdb, DVE.tok()] if k == 0 else []
                        mm(ps_b[:, 0:1],
                           wffn2_b[:, k * 256 + mi * 128: k * 256 + (mi + 1) * 128],
                           bcolb[:, k:k + 1], k == 0, k == 7,
                           waits=w0, inc=PE if k == 7 else None)
                    dve(lambda e, mi=mi: e.tensor_copy(out=acc[:, 4 + mi:5 + mi],
                        in_=ps_b[:, 0:1]), waits=[PE.tok()])
                for mi in range(2):
                    dve(lambda e, mi=mi: e.tensor_add(
                        out=acc[:, 40 + mi * 4: 42 + mi * 4],
                        in0=acc[:, 40 + mi * 4: 42 + mi * 4],
                        in1=acc[:, 42 + mi * 4: 44 + mi * 4]))
                    dve(lambda e, mi=mi: e.tensor_add(
                        out=acc[:, 40 + mi * 4: 41 + mi * 4],
                        in0=acc[:, 40 + mi * 4: 41 + mi * 4],
                        in1=acc[:, 41 + mi * 4: 42 + mi * 4]))
                    dve(lambda e, mi=mi: e.tensor_mul(out=acc[:, 6 + mi: 7 + mi],
                        in0=acc[:, 40 + mi * 4:41 + mi * 4],
                        in1=acc[:, 4 + mi:5 + mi]))
                dve(lambda e: e.tensor_mul(out=acc[:, 16:24], in0=acc[:, 48:56],
                    in1=bcol[:, 0:8]), waits=[tcolx])
                dve(lambda e: e.tensor_mul(out=acc[:, 24:32], in0=bcol[:, 0:8],
                    in1=bcol[:, 0:8]))
                dve(lambda e: e.tensor_add(out=acc[:, 32:33], in0=acc[:, 0:1],
                    in1=acc[:, 1:2]))
                dve(lambda e: e.tensor_add(out=acc[:, 33:34], in0=acc[:, 2:3],
                    in1=acc[:, 3:4]))
                dve(lambda e: e.tensor_add(out=acc[:, 34:35], in0=acc[:, 6:7],
                    in1=acc[:, 7:8]))
                dve(lambda e: e.tensor_add(out=acc[:, 16:20], in0=acc[:, 16:20],
                    in1=acc[:, 20:24]))
                dve(lambda e: e.tensor_add(out=acc[:, 16:18], in0=acc[:, 16:18],
                    in1=acc[:, 18:20]))
                dve(lambda e: e.tensor_add(out=acc[:, 35:36], in0=acc[:, 16:17],
                    in1=acc[:, 17:18]))
                dve(lambda e: e.tensor_add(out=acc[:, 24:28], in0=acc[:, 24:28],
                    in1=acc[:, 28:32]))
                dve(lambda e: e.tensor_add(out=acc[:, 24:26], in0=acc[:, 24:26],
                    in1=acc[:, 26:28]))
                dve(lambda e: e.tensor_add(out=acc[:, 36:37], in0=acc[:, 24:25],
                    in1=acc[:, 25:26]))
                dve(lambda e: e.scalar_tensor_tensor(out=acc[:, 37:38],
                    in0=acc[:, 34:35], scalar=2.0, in1=acc[:, 32:33],
                    op0=ALU.mult, op1=ALU.add))
                dve(lambda e: e.scalar_tensor_tensor(out=acc[:, 37:38],
                    in0=acc[:, 33:34], scalar=-2.0, in1=acc[:, 37:38],
                    op0=ALU.mult, op1=ALU.add))
                dve(lambda e: e.scalar_tensor_tensor(out=acc[:, 37:38],
                    in0=acc[:, 36:37], scalar=float(T), in1=acc[:, 37:38],
                    op0=ALU.mult, op1=ALU.add))
                tsc = dve(lambda e: e.scalar_tensor_tensor(out=acc[:, 37:38],
                          in0=acc[:, 35:36], scalar=-2.0, in1=acc[:, 37:38],
                          op0=ALU.mult, op1=ALU.add))
                t0 = mm(ps_b[0:1, 0:1], ones_f[:, 0:1], acc[:, 37:38], True, True,
                        waits=[tsc], inc=PE)
                dve(lambda e, ei=ei: e.tensor_copy(out=lrow[0:1, ei:ei + 1],
                    in_=ps_b[0:1, 0:1]), waits=[t0])
            tloss = DVE.tok()
            dbg_store("losses", lrow[0:1, 0:8], [1, 8], F32)

            dve(lambda e: e.tensor_scalar_mul(out=lrow[0:1, 16:24],
                in0=lrow[0:1, 0:8], scalar1=-1.0), waits=[tloss])
            dve(lambda e: e.reduce_max(out=lrow[0:1, 8:9], in_=lrow[0:1, 16:24],
                axis=AX))
            dve(lambda e: e.tensor_scalar_mul(out=lrow[0:1, 8:9],
                in0=lrow[0:1, 8:9], scalar1=-1.0))
            dve(lambda e: e.tensor_scalar(out=lrow[0:1, 24:32], in0=lrow[0:1, 0:8],
                scalar1=lrow[0:1, 8:9], scalar2=None, op0=ALU.is_le))
            dve(lambda e: e.memset(lrow[0:1, 56:64], 0.0))
            for i in range(1, 8):
                dve(lambda e, i=i: e.tensor_add(out=lrow[0:1, 56 + i:57 + i],
                    in0=lrow[0:1, 56 + i - 1:56 + i],
                    in1=lrow[0:1, 24 + i - 1:24 + i]))
            dve(lambda e: e.tensor_scalar(out=lrow[0:1, 48:56], in0=lrow[0:1, 56:64],
                scalar1=0.0, scalar2=None, op0=ALU.is_equal))
            tmask = dve(lambda e: e.tensor_mul(out=lrow[0:1, 64:72],
                        in0=lrow[0:1, 24:32], in1=lrow[0:1, 48:56]))
            dbg_store("mask", lrow[0:1, 64:72], [1, 8], F32)
            t0 = mm(ps_b[:, 0:8], ones_f[0:1, 0:128], lrow[0:1, 64:72], True, True,
                    waits=[tmask], inc=PE)
            tmk = dve(lambda e: e.tensor_copy(out=mask_c[:, 0:8], in_=ps_b[:, 0:8]),
                      waits=[t0])

            dve(lambda e: e.memset(wsel_d[:, :], 0.0))
            dve(lambda e: e.memset(wsel_u[:, :], 0.0))
            dve(lambda e: e.memset(bau_acc[:, :], 0.0))
            dve(lambda e: e.memset(bsd[:, :], 0.0))
            for ei in range(8):
                t0 = load(stage_f[:, 0:512],
                          wad_h[ei].rearrange("(k p) c -> p k c", k=8),
                          q=DMG, waits=[DVE.tok()])
                dve(lambda e, ei=ei: e.scalar_tensor_tensor(out=wsel_d[:, :],
                    in0=stage_f[:, 0:512], scalar=mask_c[:, ei:ei + 1],
                    in1=wsel_d[:, :], op0=ALU.mult, op1=ALU.add),
                    waits=[t0, tmk])
                t0 = load(stage2_f[0:64, 0:1024], wau_h[ei], q=DMG,
                          waits=[DVE.tok()])
                dve(lambda e, ei=ei: e.scalar_tensor_tensor(out=wsel_u[0:64, :],
                    in0=stage2_f[0:64, 0:1024], scalar=mask_c[0:64, ei:ei + 1],
                    in1=wsel_u[0:64, :], op0=ALU.mult, op1=ALU.add),
                    waits=[t0, tmk])
                t0 = load(bcol[0:64, 10:11], bad_h[ei].rearrange("(a b) -> a b", b=1),
                          waits=[DVE.tok()])
                dve(lambda e, ei=ei: e.scalar_tensor_tensor(out=bsd[:, :],
                    in0=bcol[0:64, 10:11], scalar=mask_c[0:64, ei:ei + 1],
                    in1=bsd[:, :], op0=ALU.mult, op1=ALU.add),
                    waits=[t0, tmk])
                t0 = load(stage_f[0:1, 0:1024],
                          bau_h[ei].rearrange("(b a) -> b a", b=1), waits=[DVE.tok()])
                dve(lambda e, ei=ei: e.scalar_tensor_tensor(out=bau_acc[0:1, :],
                    in0=stage_f[0:1, 0:1024], scalar=mask_c[0:1, ei:ei + 1],
                    in1=bau_acc[0:1, :], op0=ALU.mult, op1=ALU.add),
                    waits=[t0, tmk])
            twd = dve(lambda e: e.tensor_copy(out=wsel_db[:, :], in_=wsel_d[:, :]))
            tba = dve(lambda e: e.tensor_copy(out=bau_b[0:1, :], in_=bau_acc[0:1, :]))
            twu0 = dve(lambda e: e.tensor_copy(out=wsel_ub[0:64, :],
                       in_=wsel_u[0:64, :]))
            twu = p.dma(DMA, lambda e: e.dma_start(out=wsel_ub[64:65, :],
                        in_=bau_b[0:1, :]), waits=[tba, twu0])

            for half in range(4):
                for k in range(8):
                    w0 = [twd, DVE.tok()] if k == 0 else []
                    mm(ps_a[0:64, :], wsel_db[:, k * 64:(k + 1) * 64],
                       x1lnT[:, k * T + half * 512: k * T + half * 512 + 512],
                       k == 0, k == 7, waits=w0, inc=PE if k == 7 else None)
                dve(lambda e, half=half: e.tensor_scalar(
                    out=a_sb[0:64, half * 512: half * 512 + 512],
                    in0=ps_a[0:64, :], scalar1=bsd[:, 0:1], scalar2=0.0,
                    op0=ALU.add, op1=ALU.max), waits=[PE.tok()])
            dve(lambda e: e.memset(a_sb[64:65, :], 1.0))
            tada = DVE.tok()

            x1c = [None, None]
            ostc = [None, None]
            for tt in range(16):
                mm(ps_y[:, 0:512], a_sb[0:65, tt * 128:(tt + 1) * 128],
                   wsel_ub[0:65, 0:512], True, True,
                   waits=[tada, twu, DVE.tok(), ACT.tok()])
                t0 = mm(ps_y[:, 512:1024], a_sb[0:65, tt * 128:(tt + 1) * 128],
                        wsel_ub[0:65, 512:1024], True, True, inc=PE)
                to2 = dve(lambda e, tt=tt: e.tensor_add(out=o2_tt[:, :],
                          in0=ps_y[:, 0:D], in1=base_tm[:, tt * D:(tt + 1) * D]),
                          waits=[t0, tbase])
                dve(lambda e: e.reduce_sum(out=st_c2[:, 0:1], in_=o2_tt[:, :],
                    axis=AX), waits=[to2])
                act(lambda e: e.activation(out=sq2[:, :], in_=o2_tt[:, :],
                    func=AF.Square, accum_out=st_c2[:, 1:2]), waits=[to2])
                dve(lambda e: e.tensor_scalar_mul(out=st_c2[:, 2:3],
                    in0=st_c2[:, 0:1], scalar1=1.0 / D), waits=[ACT.tok()])
                dve(lambda e: e.tensor_scalar_mul(out=st_c2[:, 3:4],
                    in0=st_c2[:, 1:2], scalar1=1.0 / D))
                dve(lambda e: e.tensor_mul(out=st_c2[:, 4:5], in0=st_c2[:, 2:3],
                    in1=st_c2[:, 2:3]))
                tv = dve(lambda e: e.tensor_sub(out=st_c2[:, 3:4],
                         in0=st_c2[:, 3:4], in1=st_c2[:, 4:5]))
                ta = act(lambda e: e.activation(out=st_c2[:, 3:4],
                         in_=st_c2[:, 3:4], func=AF.Ln, bias=1e-5), waits=[tv])
                ta = act(lambda e: e.activation(out=st_c2[:, 3:4],
                         in_=st_c2[:, 3:4], func=AF.Exp, scale=-0.5))
                tl4 = dve(lambda e: e.tensor_scalar(out=o2_tt[:, :],
                          in0=o2_tt[:, :], scalar1=st_c2[:, 2:3],
                          scalar2=st_c2[:, 3:4], op0=ALU.subtract, op1=ALU.mult),
                          waits=[ta])
                slot = tt % 2
                w0 = ([x1c[slot]] if x1c[slot] else []) + DMG.all_toks()
                trl = load(x1_rl[:, slot * D:(slot + 1) * D],
                           x1_park[tt * 128:(tt + 1) * 128, :], waits=w0)
                tout = dve(lambda e, slot=slot: e.tensor_add(
                    out=ot_f[:, slot * D:(slot + 1) * D],
                    in0=x1_rl[:, slot * D:(slot + 1) * D], in1=o2_tt[:, :]),
                    waits=[trl, tl4] +
                          ([ostc[slot]] if ostc[slot] else []))
                x1c[slot] = tout
                ostc[slot] = store(out_h[tt * 128:(tt + 1) * 128, :],
                                   ot_f[:, slot * D:(slot + 1) * D], waits=[tout])

        p.op("sync", lambda e: e.nop(), waits=ST.all_toks())

        if TRUNC is not None:
            p.ops = p.ops[:TRUNC]
        p.run(blk)

    return nc, dbg_list



def _host_prep(inputs):
    f = np.float32
    w = {}
    w["winT"] = np.ascontiguousarray(inputs["L0_attn_in_proj_w"].T.astype(f))
    cw = inputs["L0_attn_conv_w"].astype(f)
    cd = np.zeros((4, NDT, 128, 128), dtype=f)
    for jj in range(4):
        for j in range(NDT):
            np.fill_diagonal(cd[jj, j], cw[j * 128:(j + 1) * 128, jj])
    w["convdiag"] = cd
    dd = np.zeros((NDT, 128, 128), dtype=f)
    for j in range(NDT):
        np.fill_diagonal(dd[j], inputs["L0_attn_D_param"][j * 128:(j + 1) * 128].astype(f))
    w["ddiag"] = dd
    w["ident"] = np.eye(128, dtype=f)
    sel = np.zeros((NS, NS, 128), dtype=f)
    for n in range(NS):
        sel[n, n, :] = 1.0
    w["sel16"] = sel
    xp = inputs["L0_attn_x_proj_w"].astype(f)
    pad = np.zeros((DI, 128), dtype=f)
    pad[:, 0:64] = xp[0:64].T
    pad[:, 64:80] = xp[64:80].T
    pad[:, 96:112] = xp[80:96].T
    w["xpjT"] = pad
    w["dtT"] = np.ascontiguousarray(inputs["L0_attn_dt_proj_w"].T.astype(f))
    w["woutT"] = np.ascontiguousarray(inputs["L0_attn_out_proj_w"].T.astype(f))
    w["alog"] = inputs["L0_attn_A_log"].astype(f)
    w["convb"] = inputs["L0_attn_conv_b"].astype(f)
    w["dtb"] = inputs["L0_attn_dt_proj_b"].astype(f)
    w["wbase"] = inputs["L0_ffn_W_base"].astype(f)
    w["wd1"] = inputs["L0_ffn_Wd1"].astype(f)
    w["wd2T"] = np.ascontiguousarray(np.transpose(inputs["L0_ffn_Wd2"], (0, 2, 1)).astype(f))
    w["wad"] = inputs["L0_ffn_Wa_down"].astype(f)
    w["wau"] = inputs["L0_ffn_Wa_up"].astype(f)
    w["bbase"] = inputs["L0_ffn_b_base"].astype(f)
    w["bd1"] = inputs["L0_ffn_bd1"].astype(f)
    w["bd2"] = inputs["L0_ffn_bd2"].astype(f)
    w["bad"] = inputs["L0_ffn_ba_down"].astype(f)
    w["bau"] = inputs["L0_ffn_ba_up"].astype(f)
    return w


def kernel(**inputs):
    x = inputs["L0_attn_hidden_states"].astype(np.float32)
    w = _host_prep(inputs)
    nc, _ = build_program(dbg=DEBUG)
    in_maps = []
    for b in range(8):
        m = dict(w)
        m["x0"] = np.ascontiguousarray(x[b])
        in_maps.append(m)
    res = run_bass_kernel_spmd(nc, in_maps, core_ids=list(range(8)))
    out = np.stack([r["out"] for r in res.results]).astype(np.float32)
    return out


# version 14 (final)
# speedup vs baseline: 1.1924x; 1.1924x over previous
import numpy as np
import contextlib
import concourse.bass as bass
import concourse.mybir as mybir
from concourse.bass_utils import run_bass_kernel_spmd

F32 = mybir.dt.float32
BF16 = mybir.dt.bfloat16
AF = mybir.ActivationFunctionType
ALU = mybir.AluOpType
AX = mybir.AxisListType.X

T = 2048
D = 1024
DI = 2048
NS = 16
TC = 1024
NCH = T // TC
NDT = DI // 128
DEBUG = False
TRUNC = None


class Chan:

    def __init__(self, nc, ctx, name, step, engine_name):
        self.sem = ctx.enter_context(nc.semaphore(name))
        self.step = step
        self.engine_name = engine_name
        self.n = 0

    def tok(self):
        return (self, self.n)


class SubSem:
    def __init__(self, sem, engine_name):
        self.sem = sem
        self.step = 16
        self.engine_name = engine_name
        self.n = 0

    def tok(self):
        return (self, self.n)


class QPool:

    K = 6

    def __init__(self, nc, ctx, name, engine_name):
        self.engine_name = engine_name
        self.subs = [SubSem(ctx.enter_context(nc.semaphore(f"{name}{i}")),
                            "dma-completion") for i in range(self.K)]
        self.i = 0

    def next(self):
        sub = self.subs[self.i % self.K]
        self.i += 1
        throttle = sub.tok() if sub.n > 0 else None
        sub.n += 1
        return sub, throttle

    def all_toks(self):
        return [s.tok() for s in self.subs if s.n > 0]


class Prog:
    def __init__(self):
        self.ops = []

    def op(self, engine, emit, waits=(), inc=None, inc_n=1):
        ws = tuple((c, v) for (c, v) in waits if v > 0)
        incs = ()
        if inc is not None:
            inc.n += inc_n
            incs = ((inc, inc_n),)
        self.ops.append((engine, ws, emit, incs))
        return inc.tok() if inc else None

    def dma(self, q, emit, waits=()):
        sub, throttle = q.next()
        ws = list(waits)
        if throttle is not None:
            ws.append(throttle)
        ws = tuple((c, v) for (c, v) in ws if v > 0)
        self.ops.append((q.engine_name, ws, emit, ((sub, 1),)))
        return sub.tok()

    def section(self, eng_name):
        own = getattr(self, "own", {}).get(eng_name)

        def _(eng):
            waited = {}
            seen_own = 0
            for engine, ws, emit, incs in self.ops:
                if engine != eng_name:
                    continue
                if own is not None and seen_own > waited.get(id(own), -1):
                    eng.wait_ge(own.sem, own.step * seen_own)
                    waited[id(own)] = seen_own
                for (chn, val) in ws:
                    if chn.engine_name == eng_name:
                        continue
                    if waited.get(id(chn), -1) >= val:
                        continue
                    eng.wait_ge(chn.sem, chn.step * val)
                    waited[id(chn)] = val
                instr = emit(eng)
                for (chn, k) in incs:
                    instr.then_inc(chn.sem, chn.step * k)
                    if chn is own:
                        seen_own += k
        return _

    def run(self, block):
        for name in ("sync", "gpsimd", "vector", "scalar", "tensor"):
            getattr(block, name)(self.section(name))


def build_program(dbg=False, parks_external=False):
    nc = bass.Bass("TRN2", target_bir_lowering=False)

    def din(name, shape, dt=F32):
        return nc.dram_tensor(name, list(shape), dt, kind="ExternalInput").ap()

    x0_h = din("x0", [T, D])
    winT_h = din("winT", [D, 2 * DI])
    convdiag_h = din("convdiag", [4, NDT, 128, 128])
    dparam_h = din("dparam", [DI])
    ident_h = din("ident", [128, 128])
    sel16_h = din("sel16", [NS, NS, 128])
    xpjT_h = din("xpjT", [DI, 128])
    dtT_h = din("dtT", [64, DI])
    woutT_h = din("woutT", [DI, D])
    alog_h = din("alog", [DI, NS])
    convb_h = din("convb", [DI])
    dtb_h = din("dtb", [DI])
    wbase_h = din("wbase", [D, D])
    wd1_h = din("wd1", [8, D, 256])
    wd2T_h = din("wd2T", [8, D, 256])
    wad_h = din("wad", [8, D, 64])
    wau_h = din("wau", [8, 64, D])
    bbase_h = din("bbase", [D])
    bd1_h = din("bd1", [8, 256])
    bd2_h = din("bd2", [8, D])
    bad_h = din("bad", [8, 64])
    bau_h = din("bau", [8, D])

    out_h = nc.dram_tensor("out", [T, D], F32, kind="ExternalOutput").ap()
    pk = "ExternalOutput" if parks_external else "Internal"
    u_park = nc.dram_tensor("u_park", [DI, T], BF16, kind=pk).ap()
    g_park = nc.dram_tensor("g_park", [DI, T], BF16, kind=pk).ap()
    y2_park = nc.dram_tensor("y2_park", [DI, T], BF16, kind=pk).ap()
    x1_park = nc.dram_tensor("x1_park", [T, D], BF16, kind=pk).ap()
    xln_park = nc.dram_tensor("xln_park", [D, T], BF16, kind=pk).ap()

    dbg_list = []

    with contextlib.ExitStack() as ctx:
        def sb(name, shape, dt):
            return ctx.enter_context(nc.sbuf_tensor(name, list(shape), dt))

        def psum(name, shape, dt=F32):
            return ctx.enter_context(nc.psum_tensor(name, list(shape), dt))

        p = Prog()
        DMA = QPool(nc, ctx, "c_dma", "sync")
        DMG = QPool(nc, ctx, "c_dmg", "gpsimd")
        DVE = Chan(nc, ctx, "c_dve", 1, "vector")
        ACT = Chan(nc, ctx, "c_act", 1, "scalar")
        PE = Chan(nc, ctx, "c_pe", 1, "tensor")
        ST = QPool(nc, ctx, "c_st", "sync")
        GPS = Chan(nc, ctx, "c_gps", 1, "gpsimd")
        p.own = {"vector": DVE, "scalar": ACT, "gpsimd": GPS}

        def dve(emit, waits=(), inc=DVE):
            return p.op("vector", emit, waits=waits, inc=inc)

        def act(emit, waits=(), inc=ACT):
            return p.op("scalar", emit, waits=waits, inc=inc)

        def gps(emit, waits=(), inc=GPS):
            return p.op("gpsimd", emit, waits=waits, inc=inc)

        def ccopy(kind, out_ap, in_ap, waits=()):
            if kind == "act":
                return act(lambda e, o=out_ap, i=in_ap: e.copy(out=o, in_=i),
                           waits=waits)
            if kind == "gps":
                return gps(lambda e, o=out_ap, i=in_ap:
                           e.tensor_copy(out=o, in_=i), waits=waits)
            return dve(lambda e, o=out_ap, i=in_ap:
                       e.tensor_copy(out=o, in_=i), waits=waits)

        def pe(emit, waits=(), inc=None):
            return p.op("tensor", emit, waits=waits, inc=inc)

        def load(dst, src, q=DMA, waits=()):
            return p.dma(q, lambda e, d=dst, s=src:
                         e.dma_start(out=d, in_=s), waits=waits)

        def store(dst, src, waits=(), q_eng=None, ch=ST):
            return p.dma(ch, lambda e, d=dst, s=src:
                         e.dma_start(out=d, in_=s), waits=waits)

        def mm(out, lhsT, rhs, start, stop, waits=(), inc=None):
            return p.op("tensor", lambda e, o=out, l=lhsT, r=rhs, a=start, b=stop:
                        e.matmul(out=o, lhsT=l, rhs=r, start=a, stop=b),
                        waits=waits, inc=inc)

        def tr(out, in_, identity, waits=(), inc=None):
            return p.op("tensor", lambda e, o=out, i=in_, idn=identity:
                        e.transpose(out=o, in_=i, identity=idn),
                        waits=waits, inc=inc)

        def dbg_store(name, src, shape, dt=F32):
            if not dbg:
                return
            h = nc.dram_tensor("dbg_" + name, list(shape), dt,
                               kind="ExternalOutput").ap()
            dbg_list.append("dbg_" + name)
            store(h, src, waits=[DVE.tok(), ACT.tok(), PE.tok()])

        ident_b = sb("ident_b", [128, 128], BF16)
        ones_b = sb("ones_b", [128, 128], BF16)
        ones_f = sb("ones_f", [128, 128], F32)
        sel_b = sb("sel_b", [NS, NS * 128], BF16)
        xpj_b = sb("xpj_b", [128, NDT * 128], BF16)
        dtw_f = sb("dtw_f", [64, DI], F32)
        acol = sb("acol", [128, NDT * NS], F32)
        cb_col = sb("cb_col", [128, NDT], F32)
        dtb_col = sb("dtb_col", [128, NDT], F32)
        dcol = sb("dcol", [128, NDT], F32)
        stage_f = sb("stage_f", [128, 2048], F32)
        stage2_f = sb("stage2_f", [128, 2048], F32)
        stage_c = sb("stage_c", [128, 512], F32)
        sq1 = sb("sq1", [128, 1024], F32)

        ps_a = psum("ps_a", [128, 512], F32)
        ps_conv = psum("ps_conv", [128, 512], F32)
        ps_b = psum("ps_b", [128, 1024], F32)
        ps_y = psum("ps_y", [128, 1024], F32)
        ps_tm = psum("ps_tm", [128, 1024], BF16)

        blk = ctx.enter_context(nc.Block())

        def load_cast(dst, src, stg, cols, pdim=128):
            t0 = load(stg[0:pdim, 0:cols], src, waits=[DVE.tok()])
            return dve(lambda e, d=dst, s=stg[0:pdim, 0:cols]:
                       e.tensor_copy(out=d, in_=s), waits=[t0])

        const_f = sb("const_f", [128, 4], F32)
        dve(lambda e: e.memset(const_f[:, 0:1], 1.0))
        dve(lambda e: e.memset(const_f[:, 1:2], -1.0))
        dve(lambda e: e.memset(const_f[:, 2:3], -0.5))
        dve(lambda e: e.memset(const_f[:, 3:4], 1e-5))
        nc.const_aps.register(F32, 1.0, const_f[:, 0:1])
        nc.const_aps.register(F32, -1.0, const_f[:, 1:2])
        nc.const_aps.register(F32, -0.5, const_f[:, 2:3])
        nc.const_aps.register(F32, 1e-5, const_f[:, 3:4])
        load_cast(ident_b[:, :], ident_h, stage_f, 128)
        dve(lambda e: e.memset(ones_b[:, :], 1.0))
        dve(lambda e: e.memset(ones_f[:, :], 1.0))
        for n in range(NS):
            load_cast(sel_b[0:NS, n * 128:(n + 1) * 128], sel16_h[n], stage2_f,
                      128, pdim=NS)
        for k in range(NDT):
            load_cast(xpj_b[:, k * 128:(k + 1) * 128],
                      xpjT_h[k * 128:(k + 1) * 128, :], stage_f, 128)
        load(dtw_f[:, :], dtT_h)
        for j in range(NDT):
            t0 = load(stage_f[:, 0:NS], alog_h[j * 128:(j + 1) * 128, :],
                      waits=[ACT.tok(), DVE.tok()])
            t1 = act(lambda e, j=j: e.activation(out=stage2_f[:, 0:NS],
                     in_=stage_f[:, 0:NS], func=AF.Exp), waits=[t0])
            dve(lambda e, j=j: e.tensor_scalar_mul(
                out=acol[:, j * NS:(j + 1) * NS], in0=stage2_f[:, 0:NS],
                scalar1=-1.0), waits=[t1])
        for j in range(NDT):
            load(cb_col[:, j:j + 1],
                 convb_h[j * 128:(j + 1) * 128].rearrange("(a b) -> a b", b=1))
            load(dtb_col[:, j:j + 1],
                 dtb_h[j * 128:(j + 1) * 128].rearrange("(a b) -> a b", b=1))
        for j in range(NDT):
            load(dcol[:, j:j + 1],
                 dparam_h[j * 128:(j + 1) * 128].rearrange("(a b) -> a b", b=1))
        stg_cons = [DVE.tok(), DVE.tok()]
        stc_cons = None

        with contextlib.ExitStack() as mctx:
            def msb(name, shape, dt):
                return mctx.enter_context(nc.sbuf_tensor(name, list(shape), dt))

            x0_tt = msb("x0_tt", [128, 2 * D], F32)
            st_c = msb("st_c", [128, 16], F32)
            xlnT = msb("xlnT", [128, 8 * TC], BF16)
            win_bb = msb("win_bb", [128, 8 * 512], BF16)
            convd_t = msb("convd_t", [128, 2 * 4 * 128], BF16)
            xpad = msb("xpad", [128, 2 * (TC + 3)], BF16)
            tails = msb("tails", [128, NDT * 3], BF16)
            u_t = msb("u_t", [128, 2 * TC], BF16)
            g_t = msb("g_t", [128, 2 * TC], BF16)
            dtlo = msb("dtlo", [64, TC], F32)
            b_rawb = msb("b_rawb", [16, TC], BF16)
            c_rawb = msb("c_rawb", [16, TC], BF16)
            brep = msb("brep", [128, NS * TC], BF16)
            crep = msb("crep", [128, NS * TC], BF16)
            delta = msb("delta", [128, TC], F32)
            ud_t = msb("ud_t", [128, TC], BF16)
            scr_b = msb("scr_b", [128, TC], BF16)
            g_rl = msb("g_rl", [128, 2 * TC], BF16)
            v_t = msb("v_t", [128, TC], BF16)
            da_t = msb("da_t", [128, 2 * TC], F32)
            w_t = msb("w_t", [128, 2 * TC], BF16)
            h_t = msb("h_t", [128, 2 * TC], BF16)
            hc_t = msb("hc_t", [128, 2 * TC], BF16)
            hstate = msb("hstate", [128, NDT * NS], F32)
            y2_t = msb("y2_t", [128, 2 * TC], BF16)
            y2tl = msb("y2tl", [128, 2 * 128], BF16)
            xl_fm = msb("xl_fm", [128, D], BF16)
            o_tm = msb("o_tm", [128, D], BF16)
            x1_tt = msb("x1_tt", [128, D], BF16)

            dve(lambda e: e.memset(hstate[:, :], 0.0))
            dve(lambda e: e.memset(tails[:, :], 0.0))

            xcons = [None, None]
            u_slot_prev = [None, None]
            g_slot_prev = [None, None]
            y2_slot_prev = [None, None]
            x1st_prev = None
            prev_sc = None
            prev_thc = None
            prev_hcp = None
            trc_hist = [None, None]
            hst_tok = {}
            tv2_last = None
            tw_last = None
            tdel_last = None
            tab3_last = None
            urc = [None, None]
            grc = [None, None]
            for c in range(NCH):
                t0c = c * TC
                ntt = TC // 128

                xln_done = []
                for tt in range(ntt):
                    slot = tt % 2
                    w0 = [xcons[slot]] if xcons[slot] else []
                    tok = load(x0_tt[:, slot * D:(slot + 1) * D],
                               x0_h[t0c + tt * 128: t0c + (tt + 1) * 128, :],
                               waits=w0)
                    src = x0_tt[:, slot * D:(slot + 1) * D]
                    dve(lambda e, s=src: e.reduce_sum(out=st_c[:, 0:1], in_=s,
                        axis=AX), waits=[tok])
                    act(lambda e, s=src: e.activation(out=sq1[:, :], in_=s,
                        func=AF.Square, accum_out=st_c[:, 1:2]), waits=[tok])
                    dve(lambda e: e.tensor_scalar_mul(out=st_c[:, 2:3],
                        in0=st_c[:, 0:1], scalar1=1.0 / D), waits=[ACT.tok()])
                    dve(lambda e: e.tensor_scalar_mul(out=st_c[:, 3:4],
                        in0=st_c[:, 1:2], scalar1=1.0 / D))
                    dve(lambda e: e.tensor_mul(out=st_c[:, 4:5], in0=st_c[:, 2:3],
                        in1=st_c[:, 2:3]))
                    tv = dve(lambda e: e.tensor_sub(out=st_c[:, 3:4],
                             in0=st_c[:, 3:4], in1=st_c[:, 4:5]))
                    ta = act(lambda e: e.activation(out=st_c[:, 3:4],
                             in_=st_c[:, 3:4], func=AF.Ln, bias=1e-5), waits=[tv])
                    ta = act(lambda e: e.activation(out=st_c[:, 3:4],
                             in_=st_c[:, 3:4], func=AF.Exp, scale=-0.5))
                    tap = dve(lambda e, s=src: e.tensor_scalar(out=o_tm[:, :],
                              in0=s, scalar1=st_c[:, 2:3], scalar2=st_c[:, 3:4],
                              op0=ALU.subtract, op1=ALU.mult), waits=[ta])
                    xcons[slot] = tap
                    for j in range(8):
                        tt_tok = tr(ps_tm[:, j * 128:(j + 1) * 128],
                                    o_tm[0:128, j * 128:(j + 1) * 128],
                                    ident_b[:, :],
                                    waits=([tap, DVE.tok()] + ST.all_toks()) if j == 0 else [],
                                    inc=PE if j == 7 else None)
                    for j in range(8):
                        xd = dve(lambda e, j=j, tt=tt: e.tensor_copy(
                            out=xlnT[:, j * TC + tt * 128: j * TC + (tt + 1) * 128],
                            in_=ps_tm[:, j * 128:(j + 1) * 128]),
                            waits=[tt_tok] if j == 0 else [])
                    xln_done.append(xd)
                txln = DVE.tok()
                if c == 0:
                    dbg_store("xlnT_c0", xlnT[:, :], [128, 8 * TC], BF16)
                if c == 1:
                    dbg_store("xlnT_c1", xlnT[:, :], [128, 8 * TC], BF16)

                uc = [None] * NDT
                xp_done = None
                conv_loaded = [None, None]
                for m in range(NDT):
                    mg, mi = divmod(m, 4)
                    mslot = m % 2
                    if mi == 0:
                        wb_toks = []
                        for kg in range(2):
                            stg = stage_f if kg == 0 else stage2_f
                            t0 = load(stg[:, 0:2048],
                                      winT_h[kg * 512:(kg + 1) * 512,
                                             mg * 512:(mg + 1) * 512]
                                      .rearrange("(k p) c -> p k c", k=4),
                                      q=DMG,
                                      waits=[stg_cons[kg]] if stg_cons[kg] else [])
                            tok = ccopy("dve" if kg == 0 else "gps",
                                        win_bb[:, kg * 2048:(kg + 1) * 2048],
                                        stg[:, 0:2048], waits=[t0, PE.tok()])
                            stg_cons[kg] = tok
                            wb_toks.append(tok)
                        wb_toks = [wb_toks[0]] * 4 + [wb_toks[1]] * 4
                    t0 = load(stage_c[:, 0:512],
                              convdiag_h[:, m].rearrange("j p c -> p j c"),
                              q=DMG, waits=[stc_cons] if stc_cons else [])
                    stc_cons = ccopy("dve" if m % 2 == 0 else "gps",
                                     convd_t[:, mslot * 512:(mslot + 1) * 512],
                                     stage_c[:, 0:512], waits=[t0, PE.tok()])
                    ct = [stc_cons]
                    tl0 = dve(lambda e, m=m, mslot=mslot: e.tensor_copy(
                        out=xpad[:, mslot * (TC + 3): mslot * (TC + 3) + 3],
                        in_=tails[:, m * 3:(m + 1) * 3]),
                        waits=[u_slot_prev[mslot]] if u_slot_prev[mslot] else [])
                    for half in range(2):
                        for k in range(8):
                            w0 = ([wb_toks[k], txln, DVE.tok(), ACT.tok()]
                                  if k == 0 else [wb_toks[k]])
                            mm(ps_a[:, :],
                               win_bb[:, k * 512 + mi * 128: k * 512 + (mi + 1) * 128],
                               xlnT[:, k * TC + half * 512: k * TC + half * 512 + 512],
                               k == 0, k == 7, waits=w0,
                               inc=PE if k == 7 else None)
                        dve(lambda e, mslot=mslot, half=half: e.tensor_copy(
                            out=xpad[:, mslot * (TC + 3) + 3 + half * 512:
                                     mslot * (TC + 3) + 3 + half * 512 + 512],
                            in_=ps_a[:, :]), waits=[PE.tok(), tl0])
                    tev = DVE.tok()
                    dve(lambda e, m=m, mslot=mslot: e.tensor_copy(
                        out=tails[:, m * 3:(m + 1) * 3],
                        in_=xpad[:, mslot * (TC + 3) + TC: mslot * (TC + 3) + TC + 3]))
                    for half in range(2):
                        for jj in range(4):
                            mm(ps_conv[:, :],
                               convd_t[:, (mslot * 4 + jj) * 128:
                                       (mslot * 4 + jj + 1) * 128],
                               xpad[:, mslot * (TC + 3) + jj + half * 512:
                                    mslot * (TC + 3) + jj + half * 512 + 512],
                               jj == 0, jj == 3,
                               waits=([tev] + ct if jj == 0 else []) +
                                     ([ACT.tok()] if jj == 0 else []),
                               inc=PE if jj == 3 else None)
                        act(lambda e, mslot=mslot, half=half, m=m:
                            e.activation(out=u_t[:, mslot * TC + half * 512:
                                                 mslot * TC + half * 512 + 512],
                                         in_=ps_conv[:, :], func=AF.Silu,
                                         bias=cb_col[:, m:m + 1]),
                            waits=[PE.tok()] +
                                  (list(urc[mslot]) if urc[mslot] else []))
                    tu = ACT.tok()
                    for half in range(2):
                        xp_done = mm(ps_b[:, half * 512: half * 512 + 512],
                                     xpj_b[:, m * 128:(m + 1) * 128],
                                     u_t[:, mslot * TC + half * 512:
                                         mslot * TC + half * 512 + 512],
                                     m == 0, m == NDT - 1,
                                     waits=[tu, DVE.tok()] if half == 0 else [],
                                     inc=PE if (m == NDT - 1 and half == 1) else None)
                    uc[m] = store(u_park[m * 128:(m + 1) * 128,
                                         t0c:t0c + TC],
                                  u_t[:, mslot * TC:(mslot + 1) * TC],
                                  waits=[tu], q_eng="gpsimd", ch=DMG)
                    u_slot_prev[mslot] = uc[m]

                gc = [None] * NDT
                for m in range(NDT, 2 * NDT):
                    mg, mi = divmod(m, 4)
                    mslot = m % 2
                    if mi == 0:
                        wb_toks = []
                        for kg in range(2):
                            stg = stage_f if kg == 0 else stage2_f
                            t0 = load(stg[:, 0:2048],
                                      winT_h[kg * 512:(kg + 1) * 512,
                                             mg * 512:(mg + 1) * 512]
                                      .rearrange("(k p) c -> p k c", k=4),
                                      q=DMG,
                                      waits=[stg_cons[kg]] if stg_cons[kg] else [])
                            tok = ccopy("dve" if kg == 0 else "gps",
                                        win_bb[:, kg * 2048:(kg + 1) * 2048],
                                        stg[:, 0:2048], waits=[t0, PE.tok()])
                            stg_cons[kg] = tok
                            wb_toks.append(tok)
                        wb_toks = [wb_toks[0]] * 4 + [wb_toks[1]] * 4
                    for half in range(2):
                        for k in range(8):
                            w0 = [wb_toks[k]]
                            if k == 0:
                                w0 += [DVE.tok(), ACT.tok()]
                                if g_slot_prev[mslot]:
                                    w0.append(g_slot_prev[mslot])
                            mm(ps_a[:, :],
                               win_bb[:, k * 512 + mi * 128: k * 512 + (mi + 1) * 128],
                               xlnT[:, k * TC + half * 512: k * TC + half * 512 + 512],
                               k == 0, k == 7, waits=w0,
                               inc=PE if k == 7 else None)
                        act(lambda e, mslot=mslot, half=half:
                            e.activation(out=g_t[:, mslot * TC + half * 512:
                                                 mslot * TC + half * 512 + 512],
                                         in_=ps_a[:, :], func=AF.Silu),
                            waits=[PE.tok()])
                    tg = ACT.tok()
                    gc[m - NDT] = store(g_park[(m - NDT) * 128:(m - NDT + 1) * 128,
                                               t0c:t0c + TC],
                                        g_t[:, mslot * TC:(mslot + 1) * TC],
                                        waits=[tg], q_eng="gpsimd", ch=DMG)
                    g_slot_prev[mslot] = gc[m - NDT]

                tev1 = dve(lambda e: e.tensor_copy(out=dtlo[:, :], in_=ps_b[0:64, :]),
                           waits=[xp_done])
                dve(lambda e: e.tensor_copy(out=b_rawb[:, :], in_=ps_b[64:80, :]))
                tev2 = dve(lambda e: e.tensor_copy(out=c_rawb[:, :], in_=ps_b[96:112, :]))
                if c == 0:
                    dbg_store("dtlo_c0", dtlo[:, :], [64, TC], F32)
                    dbg_store("braw_c0", b_rawb[:, :], [16, TC], BF16)

                for n in range(NS):
                    for half in range(2):
                        t0 = mm(ps_conv[:, :], sel_b[0:NS, n * 128:(n + 1) * 128],
                                b_rawb[:, half * 512: half * 512 + 512], True, True,
                                waits=[tev2, DVE.tok(), ACT.tok()], inc=PE)
                        dve(lambda e, n=n, half=half: e.tensor_copy(
                            out=brep[:, n * TC + half * 512: n * TC + half * 512 + 512],
                            in_=ps_conv[:, :]), waits=[t0])
                        t0 = mm(ps_conv[:, :], sel_b[0:NS, n * 128:(n + 1) * 128],
                                c_rawb[:, half * 512: half * 512 + 512], True, True,
                                waits=[DVE.tok()], inc=PE)
                        dve(lambda e, n=n, half=half: e.tensor_copy(
                            out=crep[:, n * TC + half * 512: n * TC + half * 512 + 512],
                            in_=ps_conv[:, :]), waits=[t0])
                trep = DVE.tok()

                y2c = [None] * NDT
                for j in range(NDT):
                    js = j % 2
                    tur = load(u_t[:, js * TC:(js + 1) * TC],
                               u_park[j * 128:(j + 1) * 128, t0c:t0c + TC],
                               waits=[uc[j], u_slot_prev[js], xp_done] +
                                     (list(urc[js]) if urc[js] else []))
                    tgr = load(g_rl[:, js * TC:(js + 1) * TC],
                               g_park[j * 128:(j + 1) * 128, t0c:t0c + TC],
                               waits=[gc[j], grc[js]] if grc[js] else [gc[j]])
                    tud = dve(lambda e, j=j, js=js: e.tensor_scalar(
                        out=ud_t[:, :], in0=u_t[:, js * TC:(js + 1) * TC],
                        scalar1=dcol[:, j:j + 1], scalar2=None, op0=ALU.mult),
                        waits=[tur, PE.tok()] +
                              ([tab3_last] if tab3_last else []))
                    mm(ps_y[:, 0:512], ident_b[:, :],
                       ud_t[:, 0:512], True, False,
                       waits=[tud, DVE.tok()])
                    tyi = mm(ps_y[:, 512:1024], ident_b[:, :],
                             ud_t[:, 512:1024], True, False, inc=PE)
                    for half in range(2):
                        tdp = mm(ps_b[:, half * 512: half * 512 + 512],
                                 dtw_f[:, j * 128:(j + 1) * 128],
                                 dtlo[:, half * 512: half * 512 + 512], True, True,
                                 waits=[tev1, ACT.tok(), DVE.tok()] if half == 0 else [],
                                 inc=PE if half == 1 else None)
                    tab = act(lambda e, j=j: e.activation(
                              out=scr_b[:, :], in_=ps_b[:, 0:TC],
                              func=AF.Abs, bias=dtb_col[:, j:j + 1]),
                              waits=[tdp] + ([tdel_last] if tdel_last else []))
                    tdp0 = dve(lambda e, j=j: e.tensor_scalar(
                               out=delta[:, 0:TC], in0=ps_b[:, 0:TC],
                               scalar1=dtb_col[:, j:j + 1], scalar2=None,
                               op0=ALU.add),
                               waits=[tdp, tab] +
                                     ([tv2_last] if tv2_last else []))
                    tdp0 = dve(lambda e: e.tensor_scalar(
                               out=delta[:, 0:TC], in0=delta[:, 0:TC],
                               scalar1=0.0, scalar2=None, op0=ALU.max))
                    tab = act(lambda e: e.activation(out=ud_t[:, :],
                              in_=scr_b[:, :], func=AF.Exp, scale=-1.0),
                              waits=[tyi])
                    tab3 = act(lambda e: e.activation(out=scr_b[:, :],
                               in_=ud_t[:, :], func=AF.Ln, bias=1.0))
                    tdel = dve(lambda e: e.tensor_add(
                        out=delta[:, 0:TC], in0=delta[:, 0:TC],
                        in1=scr_b[:, :]), waits=[tab3])
                    tdel_last = tdel
                    tab3_last = tab3
                    if c == 0 and j == 0:
                        dbg_store("delta_c0", delta[:, 0:TC], [128, TC], F32)
                    tv2 = dve(lambda e, js=js: e.tensor_mul(
                        out=v_t[:, :],
                        in0=delta[:, 0:TC],
                        in1=u_t[:, js * TC:(js + 1) * TC]),
                        waits=[tur, tdel] + ([tw_last] if tw_last else []))
                    tv2_last = tv2
                    urc[js] = (tv2, tud)
                    cur_sc = [None] * NS
                    cur_thc = [None] * NS
                    cur_hcp = [None] * NS
                    for n in range(NS):
                        asl = n % 2
                        war_da = (cur_sc[n - 2] if n >= 2 else
                                  (prev_sc[NS - 2 + asl] if prev_sc else None))
                        war_h = [t for t in
                                 ((cur_thc[n - 2], cur_hcp[n - 2]) if n >= 2 else
                                  ((prev_thc[NS - 2 + asl], prev_hcp[NS - 2 + asl])
                                   if prev_thc else (None, None))) if t]
                        hini = hst_tok.get((j, n))
                        tda = act(lambda e, j=j, n=n, asl=asl:
                                  e.activation(out=da_t[:, asl * TC:(asl + 1) * TC],
                                               in_=delta[:, 0:TC],
                                               func=AF.Exp,
                                               scale=acol[:, j * NS + n: j * NS + n + 1]),
                                  waits=[tdel] + ([war_da] if war_da else []))
                        tw = gps(lambda e, n=n, asl=asl: e.tensor_mul(
                            out=w_t[:, asl * TC:(asl + 1) * TC],
                            in0=v_t[:, :],
                            in1=brep[:, n * TC:(n + 1) * TC]),
                            waits=[trep, tv2, tda] + ([war_da] if war_da else []))
                        tsc = dve(lambda e, j=j, n=n, asl=asl: e.tensor_tensor_scan(
                            out=h_t[:, asl * TC:(asl + 1) * TC],
                            data0=da_t[:, asl * TC:(asl + 1) * TC],
                            data1=w_t[:, asl * TC:(asl + 1) * TC],
                            initial=hstate[:, j * NS + n: j * NS + n + 1],
                            op0=ALU.mult, op1=ALU.add),
                            waits=[tw] + war_h + ([hini] if hini else []))
                        cur_sc[n] = tsc
                        cur_hcp[n] = dve(lambda e, j=j, n=n, asl=asl: e.tensor_copy(
                            out=hstate[:, j * NS + n: j * NS + n + 1],
                            in_=h_t[:, asl * TC + TC - 1: asl * TC + TC]),
                            waits=[tsc])
                        hst_tok[(j, n)] = cur_hcp[n]
                        hc_eng = dve if asl == 0 else gps
                        thc = hc_eng(lambda e, n=n, asl=asl: e.tensor_mul(
                            out=hc_t[:, asl * TC:(asl + 1) * TC],
                            in0=h_t[:, asl * TC:(asl + 1) * TC],
                            in1=crep[:, n * TC:(n + 1) * TC]),
                            waits=[tsc] + ([trc_hist[asl]] if trc_hist[asl] else []))
                        cur_thc[n] = thc
                        mm(ps_y[:, 0:512], ident_b[:, :],
                           hc_t[:, asl * TC: asl * TC + 512], False, n == NS - 1,
                           waits=[thc])
                        trc_hist[asl] = mm(ps_y[:, 512:1024], ident_b[:, :],
                           hc_t[:, asl * TC + 512: asl * TC + 1024],
                           False, n == NS - 1, inc=PE)
                    prev_sc, prev_thc, prev_hcp = cur_sc, cur_thc, cur_hcp
                    tw_last = tw
                    ty2 = dve(lambda e, js=js: e.tensor_mul(
                        out=y2_t[:, js * TC:(js + 1) * TC], in0=ps_y[:, 0:TC],
                        in1=g_rl[:, js * TC:(js + 1) * TC]),
                        waits=[PE.tok(), tgr] +
                              ([y2_slot_prev[js]] if y2_slot_prev[js] else []))
                    grc[js] = ty2
                    y2c[j] = store(y2_park[j * 128:(j + 1) * 128, t0c:t0c + TC],
                                   y2_t[:, js * TC:(js + 1) * TC],
                                   waits=[ty2], q_eng="gpsimd", ch=DMG)
                    y2_slot_prev[js] = y2c[j]
                if c == 0:
                    dbg_store("y2_c0", y2_t[:, 0:TC], [128, TC], BF16)

                two_l = [None, None]
                for k2 in range(8):
                    kg = k2 % 2
                    stg = stage_f if kg == 0 else stage2_f
                    t0 = load(stg[:, 0:2048],
                              woutT_h[k2 * 256:(k2 + 1) * 256, :]
                              .rearrange("(k p) c -> p k c", k=2),
                              q=DMG,
                              waits=[trep] + ([stg_cons[kg]] if stg_cons[kg] else []))
                    two_l[kg] = ccopy("dve" if kg == 0 else "gps",
                                      brep[:, k2 * 2 * TC:(k2 + 1) * 2 * TC],
                                      stg[:, 0:2048], waits=[t0, PE.tok()])
                    stg_cons[kg] = two_l[kg]
                two = [t for t in two_l if t]
                y2lc = [None, None]
                for tt in range(ntt):
                    for k in range(NDT):
                        ksl = k % 2
                        w0 = [y2c[k]] + two
                        if k == 0:
                            w0 += [DVE.tok(), ACT.tok()]
                        if y2lc[ksl]:
                            w0.append(y2lc[ksl])
                        tyl = load(y2tl[:, ksl * 128:(ksl + 1) * 128],
                                   y2_park[k * 128:(k + 1) * 128,
                                           t0c + tt * 128: t0c + (tt + 1) * 128],
                                   waits=w0)
                        for eb in range(2):
                            tmm = mm(ps_y[:, eb * 512: eb * 512 + 512],
                                     y2tl[:, ksl * 128:(ksl + 1) * 128],
                                     brep[:, k * TC + eb * 512: k * TC + eb * 512 + 512],
                                     k == 0, k == NDT - 1,
                                     waits=[tyl] if eb == 0 else [],
                                     inc=PE if eb == 1 else None)
                        y2lc[ksl] = tmm
                    top = PE.tok()
                    trd = dve(lambda e: e.reduce_sum(out=st_c[:, 0:1], in_=ps_y[:, 0:D],
                        axis=AX), waits=[top])
                    act(lambda e: e.activation(out=sq1[:, :], in_=ps_y[:, 0:D],
                        func=AF.Square, accum_out=st_c[:, 1:2]), waits=[top, trd])
                    dve(lambda e: e.tensor_scalar_mul(out=st_c[:, 2:3],
                        in0=st_c[:, 0:1], scalar1=1.0 / D), waits=[ACT.tok()])
                    dve(lambda e: e.tensor_scalar_mul(out=st_c[:, 3:4],
                        in0=st_c[:, 1:2], scalar1=1.0 / D))
                    dve(lambda e: e.tensor_mul(out=st_c[:, 4:5], in0=st_c[:, 2:3],
                        in1=st_c[:, 2:3]))
                    tv = dve(lambda e: e.tensor_sub(out=st_c[:, 3:4],
                             in0=st_c[:, 3:4], in1=st_c[:, 4:5]))
                    ta = act(lambda e: e.activation(out=st_c[:, 3:4],
                             in_=st_c[:, 3:4], func=AF.Ln, bias=1e-5), waits=[tv])
                    ta = act(lambda e: e.activation(out=st_c[:, 3:4],
                             in_=st_c[:, 3:4], func=AF.Exp, scale=-0.5))
                    tl2 = dve(lambda e: e.tensor_scalar(out=o_tm[:, :],
                              in0=ps_y[:, 0:D], scalar1=st_c[:, 2:3],
                              scalar2=st_c[:, 3:4], op0=ALU.subtract, op1=ALU.mult),
                              waits=[ta])
                    slot = tt % 2
                    tok = load(x0_tt[:, slot * D:(slot + 1) * D],
                               x0_h[t0c + tt * 128: t0c + (tt + 1) * 128, :],
                               waits=[xcons[slot]] if xcons[slot] else [])
                    tx1 = dve(lambda e, slot=slot: e.tensor_add(out=x1_tt[:, :],
                              in0=x0_tt[:, slot * D:(slot + 1) * D], in1=o_tm[:, :]),
                              waits=[tok, tl2] +
                                    ([x1st_prev] if x1st_prev else []))
                    xcons[slot] = tx1
                    x1st_prev = store(x1_park[t0c + tt * 128: t0c + (tt + 1) * 128, :],
                                      x1_tt[:, :], waits=[tx1], q_eng="gpsimd", ch=DMG)
                    dve(lambda e: e.reduce_sum(out=st_c[:, 0:1], in_=x1_tt[:, :],
                        axis=AX), waits=[tx1])
                    act(lambda e: e.activation(out=sq1[:, :], in_=x1_tt[:, :],
                        func=AF.Square, accum_out=st_c[:, 1:2]), waits=[tx1])
                    dve(lambda e: e.tensor_scalar_mul(out=st_c[:, 2:3],
                        in0=st_c[:, 0:1], scalar1=1.0 / D), waits=[ACT.tok()])
                    dve(lambda e: e.tensor_scalar_mul(out=st_c[:, 3:4],
                        in0=st_c[:, 1:2], scalar1=1.0 / D))
                    dve(lambda e: e.tensor_mul(out=st_c[:, 4:5], in0=st_c[:, 2:3],
                        in1=st_c[:, 2:3]))
                    tv = dve(lambda e: e.tensor_sub(out=st_c[:, 3:4],
                             in0=st_c[:, 3:4], in1=st_c[:, 4:5]))
                    ta = act(lambda e: e.activation(out=st_c[:, 3:4],
                             in_=st_c[:, 3:4], func=AF.Ln, bias=1e-5), waits=[tv])
                    ta = act(lambda e: e.activation(out=st_c[:, 3:4],
                             in_=st_c[:, 3:4], func=AF.Exp, scale=-0.5))
                    tl3 = dve(lambda e: e.tensor_scalar(out=o_tm[:, :],
                              in0=x1_tt[:, :], scalar1=st_c[:, 2:3],
                              scalar2=st_c[:, 3:4], op0=ALU.subtract, op1=ALU.mult),
                              waits=[ta])
                    for j in range(8):
                        ttr = tr(ps_tm[:, j * 128:(j + 1) * 128],
                                 o_tm[0:128, j * 128:(j + 1) * 128], ident_b[:, :],
                                 waits=[tl3, DVE.tok()] if j == 0 else [],
                                 inc=PE if j == 7 else None)
                    tfm = dve(lambda e: e.tensor_copy(out=xl_fm[:, :],
                              in_=ps_tm[:, 0:D]), waits=[ttr] + ST.all_toks())
                    for j in range(8):
                        store(xln_park[j * 128:(j + 1) * 128,
                                       t0c + tt * 128: t0c + (tt + 1) * 128],
                              xl_fm[:, j * 128:(j + 1) * 128],
                              waits=[tfm] if j == 0 else [])

        with contextlib.ExitStack() as fctx:
            def fsb(name, shape, dt):
                return fctx.enter_context(nc.sbuf_tensor(name, list(shape), dt))

            x1lnT = fsb("x1lnT", [128, 8 * T], BF16)
            base_tm = fsb("base_tm", [128, 16 * D], BF16)
            wbig = fsb("wbig", [128, 8 * 512], BF16)
            wffn_b = fsb("wffn_b", [128, 8 * 256], BF16)
            wffn2_b = fsb("wffn2_b", [128, 8 * 256], BF16)
            h_e = fsb("h_e", [128, 2 * T], BF16)
            g_e = fsb("g_e", [128, 2 * T], BF16)
            m_e = fsb("m_e", [128, 2 * 256], BF16)
            mh_e = fsb("mh_e", [128, 2 * T], BF16)
            dt_scr = fsb("dt_scr", [128, T], BF16)
            acc = fsb("acc", [128, 64], F32)
            lrow = fsb("lrow", [1, 80], F32)
            mask_c = fsb("mask_c", [128, 8], F32)
            wsel_d = fsb("wsel_d", [128, 8 * 64], F32)
            wsel_db = fsb("wsel_db", [128, 8 * 64], BF16)
            wsel_u = fsb("wsel_u", [65, D], F32)
            wsel_ub = fsb("wsel_ub", [65, D], BF16)
            bcol = fsb("bcol", [128, 12], F32)
            bcolb = fsb("bcolb", [128, 8], BF16)
            bau_acc = fsb("bau_acc", [1, D], F32)
            bau_b = fsb("bau_b", [1, D], BF16)
            bsd = fsb("bsd", [64, 1], F32)
            a_sb = fsb("a_sb", [65, T], BF16)
            o2_tt = fsb("o2_tt", [128, D], BF16)
            st_c2 = fsb("st_c2", [128, 8], F32)
            sq2 = fsb("sq2", [128, D], F32)
            ot_f = fsb("ot_f", [128, 2 * D], F32)
            x1_rl = fsb("x1_rl", [128, 2 * D], BF16)
            bbrow = fsb("bbrow", [1, D], F32)

            tx = []
            for j in range(8):
                tx.append(load(x1lnT[:, j * T:(j + 1) * T],
                               xln_park[j * 128:(j + 1) * 128, :],
                               waits=ST.all_toks() + DMG.all_toks()))
            txl = tx[-1]
            load(bbrow[0:1, :], bbase_h.rearrange("(b a) -> b a", b=1))
            tbb_list = DMA.all_toks()
            tbbb = dve(lambda e: e.tensor_copy(out=dt_scr[0:1, 0:D],
                       in_=bbrow[0:1, :]), waits=tbb_list)

            for eb in range(2):
                wb_toks = []
                for kg in range(2):
                    stg = stage_f if kg == 0 else stage2_f
                    t0 = load(stg[:, 0:2048],
                              wbase_h[kg * 512:(kg + 1) * 512,
                                      eb * 512: eb * 512 + 512]
                              .rearrange("(k p) c -> p k c", k=4),
                              q=DMG,
                              waits=[stg_cons[kg]] if stg_cons[kg] else [])
                    tok = ccopy("dve" if kg == 0 else "gps",
                                wbig[:, kg * 2048:(kg + 1) * 2048],
                                stg[:, 0:2048], waits=[t0, PE.tok()])
                    stg_cons[kg] = tok
                    wb_toks.append(tok)
                wb_toks = [wb_toks[0]] * 4 + [wb_toks[1]] * 4
                for tt in range(16):
                    for k in range(8):
                        w0 = [wb_toks[k], txl, DVE.tok()] if k == 0 else [wb_toks[k]]
                        mm(ps_y[:, eb * 512: eb * 512 + 512],
                           x1lnT[:, k * T + tt * 128: k * T + (tt + 1) * 128],
                           wbig[:, k * 512:(k + 1) * 512],
                           k == 0, False, waits=w0)
                    mm(ps_y[:, eb * 512: eb * 512 + 512], ones_b[0:1, 0:128],
                       dt_scr[0:1, eb * 512: eb * 512 + 512], False, True,
                       waits=[tbbb], inc=PE)
                    dve(lambda e, tt=tt, eb=eb: e.tensor_copy(
                        out=base_tm[:, tt * D + eb * 512: tt * D + eb * 512 + 512],
                        in_=ps_y[:, eb * 512: eb * 512 + 512]),
                        waits=[PE.tok()])
            tbase = DVE.tok()

            for j in range(8):
                dve(lambda e, j=j: e.tensor_scalar(out=dt_scr[:, 0:T],
                    in0=x1lnT[:, j * T:(j + 1) * T], scalar1=1.0, scalar2=0.0,
                    op0=ALU.mult, op1=ALU.add, accum_out=acc[:, 48 + j:49 + j]),
                    waits=[txl, tbase])
            tcolx = DVE.tok()

            for ei in range(8):
                for j in range(8):
                    load(bcol[:, j:j + 1],
                         bd2_h[ei, j * 128:(j + 1) * 128].rearrange("(a b) -> a b", b=1),
                         waits=[DVE.tok(), PE.tok()])
                load(bcol[:, 8:9], bd1_h[ei, 0:128].rearrange("(a b) -> a b", b=1))
                load(bcol[:, 9:10], bd1_h[ei, 128:256].rearrange("(a b) -> a b", b=1))
                tbd_list = DMA.all_toks()
                tbdb = dve(lambda e: e.tensor_copy(out=bcolb[:, 0:8],
                           in_=bcol[:, 0:8]), waits=tbd_list)
                t0 = load(stage_f[:, 0:2048],
                          wd1_h[ei].rearrange("(k p) c -> p k c", k=8),
                          q=DMG, waits=[stg_cons[0]] if stg_cons[0] else [])
                w1tok = ccopy("dve", wffn_b[:, :], stage_f[:, 0:2048],
                              waits=[t0, PE.tok()])
                stg_cons[0] = w1tok
                t0 = load(stage2_f[:, 0:2048],
                          wd2T_h[ei].rearrange("(k p) c -> p k c", k=8),
                          q=DMG, waits=[stg_cons[1]] if stg_cons[1] else [])
                w2tok = ccopy("gps", wffn2_b[:, :], stage2_f[:, 0:2048],
                              waits=[t0, PE.tok()])
                stg_cons[1] = w2tok
                wt1 = [w1tok] * 8
                wt2 = [w2tok] * 8
                for mi in range(2):
                    for half in range(4):
                        for k in range(8):
                            w0 = [wt1[k], DVE.tok()] if k == 0 else [wt1[k]]
                            mm(ps_a[:, :],
                               wffn_b[:, k * 256 + mi * 128: k * 256 + (mi + 1) * 128],
                               x1lnT[:, k * T + half * 512: k * T + half * 512 + 512],
                               k == 0, k == 7, waits=w0,
                               inc=PE if k == 7 else None)
                        dve(lambda e, mi=mi, half=half:
                            e.tensor_scalar(out=h_e[:, mi * T + half * 512:
                                                    mi * T + half * 512 + 512],
                                            in0=ps_a[:, :],
                                            scalar1=bcol[:, 8 + mi: 9 + mi],
                                            scalar2=0.0, op0=ALU.add, op1=ALU.max),
                            waits=[PE.tok()] + tbd_list)
                th = DVE.tok()
                for mi in range(2):
                    dve(lambda e, mi=mi: e.tensor_scalar(out=dt_scr[:, 0:T],
                        in0=h_e[:, mi * T:(mi + 1) * T], scalar1=1.0,
                        scalar2=0.0, op0=ALU.mult, op1=ALU.add,
                        accum_out=acc[:, 40 + mi * 4: 41 + mi * 4]))
                    dve(lambda e, mi=mi: e.memset(
                        acc[:, 41 + mi * 4: 44 + mi * 4], 0.0))
                for mi in range(2):
                    for half in range(4):
                        for k in range(8):
                            w0 = [wt2[k], DVE.tok()] if k == 0 else [wt2[k]]
                            mm(ps_conv[:, :],
                               wffn2_b[:, k * 256 + mi * 128: k * 256 + (mi + 1) * 128],
                               x1lnT[:, k * T + half * 512: k * T + half * 512 + 512],
                               k == 0, k == 7, waits=w0,
                               inc=PE if k == 7 else None)
                        dve(lambda e, mi=mi, half=half:
                            e.tensor_copy(out=g_e[:, mi * T + half * 512:
                                                  mi * T + half * 512 + 512],
                                          in_=ps_conv[:, :]), waits=[PE.tok()])
                tg = DVE.tok()
                for mi in range(2):
                    for k in range(8):
                        w0 = [wt2[k], DVE.tok()] if k == 0 else [wt2[k]]
                        mm(ps_b[:, 0:256],
                           wffn2_b[:, k * 256 + mi * 128: k * 256 + (mi + 1) * 128],
                           wffn2_b[:, k * 256:(k + 1) * 256],
                           k == 0, k == 7, waits=w0, inc=PE if k == 7 else None)
                    dve(lambda e, mi=mi: e.tensor_copy(
                        out=m_e[:, mi * 256:(mi + 1) * 256], in_=ps_b[:, 0:256]),
                        waits=[PE.tok()])
                tme = DVE.tok()
                for mi in range(2):
                    for half in range(4):
                        for k in range(2):
                            w0 = [tme, th, DVE.tok()] if k == 0 else []
                            mm(ps_conv[:, :],
                               m_e[:, k * 256 + mi * 128: k * 256 + (mi + 1) * 128],
                               h_e[:, k * T + half * 512: k * T + half * 512 + 512],
                               k == 0, k == 1, waits=w0, inc=PE if k == 1 else None)
                        dve(lambda e, mi=mi, half=half:
                            e.tensor_copy(out=mh_e[:, mi * T + half * 512:
                                                   mi * T + half * 512 + 512],
                                          in_=ps_conv[:, :]), waits=[PE.tok()])
                tmh = DVE.tok()
                for mi in range(2):
                    dve(lambda e, mi=mi: e.tensor_mul(out=dt_scr[:, 0:T],
                        in0=h_e[:, mi * T:(mi + 1) * T],
                        in1=mh_e[:, mi * T:(mi + 1) * T]), waits=[tmh])
                    dve(lambda e, mi=mi: e.tensor_scalar(out=dt_scr[:, 0:T],
                        in0=dt_scr[:, 0:T], scalar1=1.0, scalar2=0.0,
                        op0=ALU.mult, op1=ALU.add, accum_out=acc[:, 0 + mi:1 + mi]))
                    dve(lambda e, mi=mi: e.tensor_mul(out=dt_scr[:, 0:T],
                        in0=h_e[:, mi * T:(mi + 1) * T],
                        in1=g_e[:, mi * T:(mi + 1) * T]), waits=[tg])
                    dve(lambda e, mi=mi: e.tensor_scalar(out=dt_scr[:, 0:T],
                        in0=dt_scr[:, 0:T], scalar1=1.0, scalar2=0.0,
                        op0=ALU.mult, op1=ALU.add, accum_out=acc[:, 2 + mi:3 + mi]))
                for mi in range(2):
                    for k in range(8):
                        w0 = [tbdb, DVE.tok()] if k == 0 else []
                        mm(ps_b[:, 0:1],
                           wffn2_b[:, k * 256 + mi * 128: k * 256 + (mi + 1) * 128],
                           bcolb[:, k:k + 1], k == 0, k == 7,
                           waits=w0, inc=PE if k == 7 else None)
                    dve(lambda e, mi=mi: e.tensor_copy(out=acc[:, 4 + mi:5 + mi],
                        in_=ps_b[:, 0:1]), waits=[PE.tok()])
                for mi in range(2):
                    dve(lambda e, mi=mi: e.tensor_add(
                        out=acc[:, 40 + mi * 4: 42 + mi * 4],
                        in0=acc[:, 40 + mi * 4: 42 + mi * 4],
                        in1=acc[:, 42 + mi * 4: 44 + mi * 4]))
                    dve(lambda e, mi=mi: e.tensor_add(
                        out=acc[:, 40 + mi * 4: 41 + mi * 4],
                        in0=acc[:, 40 + mi * 4: 41 + mi * 4],
                        in1=acc[:, 41 + mi * 4: 42 + mi * 4]))
                    dve(lambda e, mi=mi: e.tensor_mul(out=acc[:, 6 + mi: 7 + mi],
                        in0=acc[:, 40 + mi * 4:41 + mi * 4],
                        in1=acc[:, 4 + mi:5 + mi]))
                dve(lambda e: e.tensor_mul(out=acc[:, 16:24], in0=acc[:, 48:56],
                    in1=bcol[:, 0:8]), waits=[tcolx])
                dve(lambda e: e.tensor_mul(out=acc[:, 24:32], in0=bcol[:, 0:8],
                    in1=bcol[:, 0:8]))
                dve(lambda e: e.tensor_add(out=acc[:, 32:33], in0=acc[:, 0:1],
                    in1=acc[:, 1:2]))
                dve(lambda e: e.tensor_add(out=acc[:, 33:34], in0=acc[:, 2:3],
                    in1=acc[:, 3:4]))
                dve(lambda e: e.tensor_add(out=acc[:, 34:35], in0=acc[:, 6:7],
                    in1=acc[:, 7:8]))
                dve(lambda e: e.tensor_add(out=acc[:, 16:20], in0=acc[:, 16:20],
                    in1=acc[:, 20:24]))
                dve(lambda e: e.tensor_add(out=acc[:, 16:18], in0=acc[:, 16:18],
                    in1=acc[:, 18:20]))
                dve(lambda e: e.tensor_add(out=acc[:, 35:36], in0=acc[:, 16:17],
                    in1=acc[:, 17:18]))
                dve(lambda e: e.tensor_add(out=acc[:, 24:28], in0=acc[:, 24:28],
                    in1=acc[:, 28:32]))
                dve(lambda e: e.tensor_add(out=acc[:, 24:26], in0=acc[:, 24:26],
                    in1=acc[:, 26:28]))
                dve(lambda e: e.tensor_add(out=acc[:, 36:37], in0=acc[:, 24:25],
                    in1=acc[:, 25:26]))
                dve(lambda e: e.scalar_tensor_tensor(out=acc[:, 37:38],
                    in0=acc[:, 34:35], scalar=2.0, in1=acc[:, 32:33],
                    op0=ALU.mult, op1=ALU.add))
                dve(lambda e: e.scalar_tensor_tensor(out=acc[:, 37:38],
                    in0=acc[:, 33:34], scalar=-2.0, in1=acc[:, 37:38],
                    op0=ALU.mult, op1=ALU.add))
                dve(lambda e: e.scalar_tensor_tensor(out=acc[:, 37:38],
                    in0=acc[:, 36:37], scalar=float(T), in1=acc[:, 37:38],
                    op0=ALU.mult, op1=ALU.add))
                tsc = dve(lambda e: e.scalar_tensor_tensor(out=acc[:, 37:38],
                          in0=acc[:, 35:36], scalar=-2.0, in1=acc[:, 37:38],
                          op0=ALU.mult, op1=ALU.add))
                t0 = mm(ps_b[0:1, 0:1], ones_f[:, 0:1], acc[:, 37:38], True, True,
                        waits=[tsc], inc=PE)
                dve(lambda e, ei=ei: e.tensor_copy(out=lrow[0:1, ei:ei + 1],
                    in_=ps_b[0:1, 0:1]), waits=[t0])
            tloss = DVE.tok()
            dbg_store("losses", lrow[0:1, 0:8], [1, 8], F32)

            dve(lambda e: e.tensor_scalar_mul(out=lrow[0:1, 16:24],
                in0=lrow[0:1, 0:8], scalar1=-1.0), waits=[tloss])
            dve(lambda e: e.reduce_max(out=lrow[0:1, 8:9], in_=lrow[0:1, 16:24],
                axis=AX))
            dve(lambda e: e.tensor_scalar_mul(out=lrow[0:1, 8:9],
                in0=lrow[0:1, 8:9], scalar1=-1.0))
            dve(lambda e: e.tensor_scalar(out=lrow[0:1, 24:32], in0=lrow[0:1, 0:8],
                scalar1=lrow[0:1, 8:9], scalar2=None, op0=ALU.is_le))
            dve(lambda e: e.memset(lrow[0:1, 56:64], 0.0))
            for i in range(1, 8):
                dve(lambda e, i=i: e.tensor_add(out=lrow[0:1, 56 + i:57 + i],
                    in0=lrow[0:1, 56 + i - 1:56 + i],
                    in1=lrow[0:1, 24 + i - 1:24 + i]))
            dve(lambda e: e.tensor_scalar(out=lrow[0:1, 48:56], in0=lrow[0:1, 56:64],
                scalar1=0.0, scalar2=None, op0=ALU.is_equal))
            tmask = dve(lambda e: e.tensor_mul(out=lrow[0:1, 64:72],
                        in0=lrow[0:1, 24:32], in1=lrow[0:1, 48:56]))
            dbg_store("mask", lrow[0:1, 64:72], [1, 8], F32)
            t0 = mm(ps_b[:, 0:8], ones_f[0:1, 0:128], lrow[0:1, 64:72], True, True,
                    waits=[tmask], inc=PE)
            tmk = dve(lambda e: e.tensor_copy(out=mask_c[:, 0:8], in_=ps_b[:, 0:8]),
                      waits=[t0])

            dve(lambda e: e.memset(wsel_d[:, :], 0.0))
            dve(lambda e: e.memset(wsel_u[:, :], 0.0))
            dve(lambda e: e.memset(bau_acc[:, :], 0.0))
            dve(lambda e: e.memset(bsd[:, :], 0.0))
            for ei in range(8):
                t0 = load(stage_f[:, 0:512],
                          wad_h[ei].rearrange("(k p) c -> p k c", k=8),
                          q=DMG,
                          waits=[DVE.tok()] +
                                ([stg_cons[0]] if ei == 0 and stg_cons[0] else []))
                dve(lambda e, ei=ei: e.scalar_tensor_tensor(out=wsel_d[:, :],
                    in0=stage_f[:, 0:512], scalar=mask_c[:, ei:ei + 1],
                    in1=wsel_d[:, :], op0=ALU.mult, op1=ALU.add),
                    waits=[t0, tmk])
                t0 = load(stage2_f[0:64, 0:1024], wau_h[ei], q=DMG,
                          waits=[DVE.tok()] +
                                ([stg_cons[1]] if ei == 0 and stg_cons[1] else []))
                dve(lambda e, ei=ei: e.scalar_tensor_tensor(out=wsel_u[0:64, :],
                    in0=stage2_f[0:64, 0:1024], scalar=mask_c[0:64, ei:ei + 1],
                    in1=wsel_u[0:64, :], op0=ALU.mult, op1=ALU.add),
                    waits=[t0, tmk])
                t0 = load(bcol[0:64, 10:11], bad_h[ei].rearrange("(a b) -> a b", b=1),
                          waits=[DVE.tok()])
                dve(lambda e, ei=ei: e.scalar_tensor_tensor(out=bsd[:, :],
                    in0=bcol[0:64, 10:11], scalar=mask_c[0:64, ei:ei + 1],
                    in1=bsd[:, :], op0=ALU.mult, op1=ALU.add),
                    waits=[t0, tmk])
                t0 = load(stage_f[0:1, 0:1024],
                          bau_h[ei].rearrange("(b a) -> b a", b=1), waits=[DVE.tok()])
                dve(lambda e, ei=ei: e.scalar_tensor_tensor(out=bau_acc[0:1, :],
                    in0=stage_f[0:1, 0:1024], scalar=mask_c[0:1, ei:ei + 1],
                    in1=bau_acc[0:1, :], op0=ALU.mult, op1=ALU.add),
                    waits=[t0, tmk])
            twd = dve(lambda e: e.tensor_copy(out=wsel_db[:, :], in_=wsel_d[:, :]))
            tba = dve(lambda e: e.tensor_copy(out=bau_b[0:1, :], in_=bau_acc[0:1, :]))
            twu0 = dve(lambda e: e.tensor_copy(out=wsel_ub[0:64, :],
                       in_=wsel_u[0:64, :]))
            twu = p.dma(DMA, lambda e: e.dma_start(out=wsel_ub[64:65, :],
                        in_=bau_b[0:1, :]), waits=[tba, twu0])

            for half in range(4):
                for k in range(8):
                    w0 = [twd, DVE.tok()] if k == 0 else []
                    mm(ps_a[0:64, :], wsel_db[:, k * 64:(k + 1) * 64],
                       x1lnT[:, k * T + half * 512: k * T + half * 512 + 512],
                       k == 0, k == 7, waits=w0, inc=PE if k == 7 else None)
                dve(lambda e, half=half: e.tensor_scalar(
                    out=a_sb[0:64, half * 512: half * 512 + 512],
                    in0=ps_a[0:64, :], scalar1=bsd[:, 0:1], scalar2=0.0,
                    op0=ALU.add, op1=ALU.max), waits=[PE.tok()])
            dve(lambda e: e.memset(a_sb[64:65, :], 1.0))
            tada = DVE.tok()

            x1c = [None, None]
            ostc = [None, None]
            for tt in range(16):
                mm(ps_y[:, 0:512], a_sb[0:65, tt * 128:(tt + 1) * 128],
                   wsel_ub[0:65, 0:512], True, True,
                   waits=[tada, twu, DVE.tok(), ACT.tok()])
                t0 = mm(ps_y[:, 512:1024], a_sb[0:65, tt * 128:(tt + 1) * 128],
                        wsel_ub[0:65, 512:1024], True, True, inc=PE)
                to2 = dve(lambda e, tt=tt: e.tensor_add(out=o2_tt[:, :],
                          in0=ps_y[:, 0:D], in1=base_tm[:, tt * D:(tt + 1) * D]),
                          waits=[t0, tbase])
                dve(lambda e: e.reduce_sum(out=st_c2[:, 0:1], in_=o2_tt[:, :],
                    axis=AX), waits=[to2])
                act(lambda e: e.activation(out=sq2[:, :], in_=o2_tt[:, :],
                    func=AF.Square, accum_out=st_c2[:, 1:2]), waits=[to2])
                dve(lambda e: e.tensor_scalar_mul(out=st_c2[:, 2:3],
                    in0=st_c2[:, 0:1], scalar1=1.0 / D), waits=[ACT.tok()])
                dve(lambda e: e.tensor_scalar_mul(out=st_c2[:, 3:4],
                    in0=st_c2[:, 1:2], scalar1=1.0 / D))
                dve(lambda e: e.tensor_mul(out=st_c2[:, 4:5], in0=st_c2[:, 2:3],
                    in1=st_c2[:, 2:3]))
                tv = dve(lambda e: e.tensor_sub(out=st_c2[:, 3:4],
                         in0=st_c2[:, 3:4], in1=st_c2[:, 4:5]))
                ta = act(lambda e: e.activation(out=st_c2[:, 3:4],
                         in_=st_c2[:, 3:4], func=AF.Ln, bias=1e-5), waits=[tv])
                ta = act(lambda e: e.activation(out=st_c2[:, 3:4],
                         in_=st_c2[:, 3:4], func=AF.Exp, scale=-0.5))
                tl4 = dve(lambda e: e.tensor_scalar(out=o2_tt[:, :],
                          in0=o2_tt[:, :], scalar1=st_c2[:, 2:3],
                          scalar2=st_c2[:, 3:4], op0=ALU.subtract, op1=ALU.mult),
                          waits=[ta])
                slot = tt % 2
                w0 = ([x1c[slot]] if x1c[slot] else []) + DMG.all_toks()
                trl = load(x1_rl[:, slot * D:(slot + 1) * D],
                           x1_park[tt * 128:(tt + 1) * 128, :], waits=w0)
                tout = dve(lambda e, slot=slot: e.tensor_add(
                    out=ot_f[:, slot * D:(slot + 1) * D],
                    in0=x1_rl[:, slot * D:(slot + 1) * D], in1=o2_tt[:, :]),
                    waits=[trl, tl4] +
                          ([ostc[slot]] if ostc[slot] else []))
                x1c[slot] = tout
                ostc[slot] = store(out_h[tt * 128:(tt + 1) * 128, :],
                                   ot_f[:, slot * D:(slot + 1) * D], waits=[tout])

        p.op("sync", lambda e: e.nop(), waits=ST.all_toks())

        if TRUNC is not None:
            p.ops = p.ops[:TRUNC]
        p.run(blk)

    return nc, dbg_list



def _host_prep(inputs):
    f = np.float32
    w = {}
    w["winT"] = np.ascontiguousarray(inputs["L0_attn_in_proj_w"].T.astype(f))
    cw = inputs["L0_attn_conv_w"].astype(f)
    cd = np.zeros((4, NDT, 128, 128), dtype=f)
    for jj in range(4):
        for j in range(NDT):
            np.fill_diagonal(cd[jj, j], cw[j * 128:(j + 1) * 128, jj])
    w["convdiag"] = cd
    w["dparam"] = inputs["L0_attn_D_param"].astype(f)
    w["ident"] = np.eye(128, dtype=f)
    sel = np.zeros((NS, NS, 128), dtype=f)
    for n in range(NS):
        sel[n, n, :] = 1.0
    w["sel16"] = sel
    xp = inputs["L0_attn_x_proj_w"].astype(f)
    pad = np.zeros((DI, 128), dtype=f)
    pad[:, 0:64] = xp[0:64].T
    pad[:, 64:80] = xp[64:80].T
    pad[:, 96:112] = xp[80:96].T
    w["xpjT"] = pad
    w["dtT"] = np.ascontiguousarray(inputs["L0_attn_dt_proj_w"].T.astype(f))
    w["woutT"] = np.ascontiguousarray(inputs["L0_attn_out_proj_w"].T.astype(f))
    w["alog"] = inputs["L0_attn_A_log"].astype(f)
    w["convb"] = inputs["L0_attn_conv_b"].astype(f)
    w["dtb"] = inputs["L0_attn_dt_proj_b"].astype(f)
    w["wbase"] = inputs["L0_ffn_W_base"].astype(f)
    w["wd1"] = inputs["L0_ffn_Wd1"].astype(f)
    w["wd2T"] = np.ascontiguousarray(np.transpose(inputs["L0_ffn_Wd2"], (0, 2, 1)).astype(f))
    w["wad"] = inputs["L0_ffn_Wa_down"].astype(f)
    w["wau"] = inputs["L0_ffn_Wa_up"].astype(f)
    w["bbase"] = inputs["L0_ffn_b_base"].astype(f)
    w["bd1"] = inputs["L0_ffn_bd1"].astype(f)
    w["bd2"] = inputs["L0_ffn_bd2"].astype(f)
    w["bad"] = inputs["L0_ffn_ba_down"].astype(f)
    w["bau"] = inputs["L0_ffn_ba_up"].astype(f)
    return w


def kernel(**inputs):
    x = inputs["L0_attn_hidden_states"].astype(np.float32)
    w = _host_prep(inputs)
    nc, _ = build_program(dbg=DEBUG)
    in_maps = []
    for b in range(8):
        m = dict(w)
        m["x0"] = np.ascontiguousarray(x[b])
        in_maps.append(m)
    res = run_bass_kernel_spmd(nc, in_maps, core_ids=list(range(8)))
    out = np.stack([r["out"] for r in res.results]).astype(np.float32)
    return out
